# version 18
# baseline (speedup 1.0000x reference)
import contextlib
import numpy as np
import jax
import jax.numpy as jnp
import concourse.bass as bass
import concourse.mybir as mybir
from concourse.bass_utils import run_bass_kernel_spmd

F32 = mybir.dt.float32
BF16 = mybir.dt.bfloat16
AF = mybir.ActivationFunctionType
ALU = mybir.AluOpType
AX = mybir.AxisListType

PI = float(np.pi)
C_HI = float(np.float32(np.pi / 4096.0))
C_LO = float(np.float32(np.pi / 4096.0 - C_HI))
PH_HI = float(np.float32(np.pi / 2.0))
PH_LO = float(np.float32(np.pi / 2.0 - PH_HI))
E = 1024
NH = 16
HD = 64
L = 2048
HALF = 1024
NCH = 8
LN_EPS = 1e-5
AT_EPS = 1e-4


def build():
    nc = bass.Bass("TRN2", target_bir_lowering=False)

    x_h = nc.dram_tensor("x", [2, L, E], F32, kind="ExternalInput").ap()
    wqkvg_h = nc.dram_tensor("wqkvg", [E, 3080], F32, kind="ExternalInput").ap()
    wout_h = nc.dram_tensor("wout", [E, E], F32, kind="ExternalInput").ap()
    bqkv_h = nc.dram_tensor("bqkv", [3072], F32, kind="ExternalInput").ap()
    bout_h = nc.dram_tensor("bout", [E], F32, kind="ExternalInput").ap()
    wexp_h = nc.dram_tensor("wexp", [8, E, E], F32, kind="ExternalInput").ap()
    trig_h = nc.dram_tensor("trig", [2, L], F32, kind="ExternalInput").ap()
    out_h = nc.dram_tensor("out", [2, L, E], F32, kind="ExternalOutput").ap()
    scr_hi = nc.dram_tensor("scr_hi", [24, 128, 4096], BF16, kind="Internal").ap()
    scr_lo = nc.dram_tensor("scr_lo", [24, 128, 4096], BF16, kind="Internal").ap()

    st = contextlib.ExitStack()

    def sb(name, shape, dt=F32):
        return st.enter_context(nc.sbuf_tensor(name, shape, dt))

    xbuf = [sb(f"xbuf{i}", [128, E]) for i in range(2)]
    xnbuf = [sb("xnbuf0", [128, E])]
    xnbuf.append(xnbuf[0])
    out_sb = xbuf
    acc = sb("acc", [128, NCH * E])
    xnh = [sb(f"xnh{i}", [128, 8 * HALF], BF16) for i in range(2)]
    xnl = [sb(f"xnl{i}", [128, 8 * HALF], BF16) for i in range(2)]
    afm_hi = sb("afm_hi", [128, 8 * 256], BF16)
    afm_lo = sb("afm_lo", [128, 8 * 256], BF16)

    whi = [sb(f"whi{i}", [128, 4096], BF16) for i in range(3)]
    wlo = [sb(f"wlo{i}", [128, 4096], BF16) for i in range(3)]

    ghi = sb("ghi", [128, 64], BF16)
    glo = sb("glo", [128, 64], BF16)

    k_sb = [sb(f"ksb{i}", [128, 256]) for i in range(2)]
    v_sb = [sb(f"vsb{i}", [128, 4 * 65]) for i in range(2)]
    kcs_all = [sb(f"kcsa{i}", [128, 512]) for i in range(2)]
    gsb = kcs_all[0]
    kvn = sb("kvn", [128, NH * 65])
    kvn_sw = sb("kvn_sw", [128, NH * 65])

    qrelu = [sb(f"qrelu{i}", [128, 256]) for i in range(2)]
    qcos = [sb(f"qcos{i}", [128, 256]) for i in range(2)]
    qsin = [sb(f"qsin{i}", [128, 256]) for i in range(2)]
    atile = [[sb(f"atile{i}_{j}", [128, 128]) for j in range(2)] for i in range(2)]
    nrm4 = sb("nrm4", [128, 8])
    rcp4 = sb("rcp4", [128, 8])

    sumt = sb("sumt", [128, 64])
    sqt = sb("sqt", [128, 64])
    sq2t = sb("sq2t", [128, 64])
    vart = sb("vart", [128, 64])
    m2t = sumt
    sstdt = sb("sstdt", [128, 64])
    mean_t = sb("mean_t", [128, 128])
    rstd_t = sb("rstd_t", [128, 128])
    masks = sb("masks", [128, 128])

    ident = sb("ident", [128, 128])
    idp = atile[0][0]
    idf = atile[0][1]
    sincols = sb("sincols", [128, 16])
    coscols = sb("coscols", [128, 16])
    coswin = sb("coswin", [128, 256])
    sinwin = sb("sinwin", [128, 256])
    ones1 = sb("ones1", [1, 128])
    eps5 = sb("eps5", [128, 1])
    bq_cols = sb("bq_cols", [128, 8])
    bkv_bc = sb("bkv_bc", [128, 2048])
    bout_bc = sb("bout_bc", [128, 1024])

    pb = [st.enter_context(nc.psum_tensor(f"pb{i}", [128, 512], F32))
          for i in range(8)]
    B_TR = (0, 1)
    B_PRJ = (2, 3)
    B_KV = (4, 5, 6)
    B_G = 7

    S = {}
    C = {}
    for name in ["sx0", "sx1", "sw0", "sw1", "sw2", "so0", "so1", "sb", "skv",
                 "swq", "swo", "sq0", "sq1", "str", "t", "v", "a", "g"]:
        S[name] = st.enter_context(nc.semaphore(name))
        C[name] = 0

    def inc(instr, name, n=1):
        instr.then_inc(S[name], n)
        C[name] += n
        return C[name]

    def w(eng, name, val):
        if val > 0:
            eng.wait_ge(S[name], val)

    free = {}

    def wait_free(eng, key):
        if key in free:
            nm, val = free[key]
            w(eng, nm, val)

    def set_free(key, nm):
        free[key] = (nm, C[nm])

    kv_started = {}

    LAST = {}

    class _EP:
        def __init__(self, name, eng):
            self._n, self._e = name, eng

        def __getattr__(self, k):
            val = getattr(self._e, k)
            if not callable(val):
                return val

            def wrapped(*a, **kw):
                r = val(*a, **kw)
                if hasattr(r, "then_inc"):
                    LAST[self._n] = r
                return r
            return wrapped

    def _wrap(name, f):
        return lambda e, f=f, name=name: f(_EP(name, e))

    blk_cm = nc.Block()
    blk = blk_cm.__enter__()
    if True:
        def gbody(g):
            inc(g.iota(idp[:], [[0, 128]], channel_multiplier=1,
                       allow_small_or_imprecise_dtypes=True), "g")
            inc(g.iota(idf[:], [[1, 128]], channel_multiplier=0,
                       allow_small_or_imprecise_dtypes=True), "g")
            src = wqkvg_h.rearrange("(fc p) c -> p fc c", p=128)[:, :, 3072:3080]
            inc(g.dma_start(out=gsb[:, 0:64].rearrange("p (fc e) -> p fc e", e=8),
                            in_=src), "sb", 16)
            inc(g.dma_start(out=acc[0:1, 0:2048],
                            in_=bqkv_h[1024:3072].rearrange("(a b) -> a b", a=1)),
                "sb", 16)
            inc(g.dma_start(out=acc[0:1, 2048:3072],
                            in_=bout_h[:].rearrange("(a b) -> a b", a=1)),
                "sb", 16)
            inc(g.dma_start(out=bq_cols[:],
                            in_=bqkv_h[0:1024].rearrange("(c p) -> p c", p=128),
                            allow_slow_non_contiguous=True),
                "sb", 16)
            inc(g.dma_start(out=coscols[:],
                            in_=trig_h[0].rearrange("(c p) -> p c", p=128),
                            allow_slow_non_contiguous=True),
                "sb", 16)
            inc(g.dma_start(out=sincols[:],
                            in_=trig_h[1].rearrange("(c p) -> p c", p=128),
                            allow_slow_non_contiguous=True),
                "sb", 16)
        blk.gpsimd(_wrap("g", gbody))
        sw_setup = C["sb"]

        def vbody(v):
            w(v, "g", 2)
            v.tensor_tensor(ident[:], idp[:], idf[:], ALU.is_equal)
            v.memset(ones1[:], 1.0)
            v.memset(eps5[:], LN_EPS)
            v.memset(v_sb[0][:].rearrange("p (h c) -> p h c", c=65)[:, :, 64:65], 1.0)
            v.drain()
            v.memset(v_sb[1][:].rearrange("p (h c) -> p h c", c=65)[:, :, 64:65], 1.0)
            inc(LAST["v"], "v")
        blk.vector(_wrap("v", vbody))
        v_consts = C["v"]

        def abody(a):
            w(a, "v", v_consts)
            w(a, "sb", sw_setup)
            a.activation(out=ghi[:], in_=gsb[:, 0:64], func=AF.Copy)
            inc(LAST["a"], "a")
        blk.scalar(_wrap("a", abody))
        a_setup = C["a"]

        def v2body(v):
            w(v, "a", a_setup)
            v.tensor_sub(glo[:], gsb[:, 0:64], ghi[:])
            inc(LAST["v"], "v")
        blk.vector(_wrap("v", v2body))

        def tbody(t):
            w(t, "sb", sw_setup)
            w(t, "v", v_consts)
            for i in range(2):
                t.matmul(pb[i][:], ones1[:, 0:128], acc[0:1, i * 512:(i + 1) * 512],
                         start=True, stop=True)
            inc(LAST["t"], "t")
        blk.tensor(_wrap("t", tbody))
        t_b1 = C["t"]

        def v3body(v):
            w(v, "t", t_b1)
            v.tensor_copy(bkv_bc[:, 0:512], pb[0][:])
            v.tensor_copy(bkv_bc[:, 512:1024], pb[1][:])
            inc(LAST["v"], "v")
        blk.vector(_wrap("v", v3body))
        v_b1 = C["v"]

        def t2body(t):
            w(t, "v", v_b1)
            for i in range(2):
                t.matmul(pb[i][:], ones1[:, 0:128],
                         acc[0:1, 1024 + i * 512:1024 + (i + 1) * 512],
                         start=True, stop=True)
            inc(LAST["t"], "t")
        blk.tensor(_wrap("t", t2body))
        t_b2 = C["t"]

        def v4body(v):
            w(v, "t", t_b2)
            v.tensor_copy(bkv_bc[:, 1024:1536], pb[0][:])
            v.tensor_copy(bkv_bc[:, 1536:2048], pb[1][:])
            inc(LAST["v"], "v")
        blk.vector(_wrap("v", v4body))
        v_b2 = C["v"]

        def t3body(t):
            w(t, "v", v_b2)
            for i in range(2):
                t.matmul(pb[i][:], ones1[:, 0:128],
                         acc[0:1, 2048 + i * 512:2048 + (i + 1) * 512],
                         start=True, stop=True)
            inc(LAST["t"], "t")
        blk.tensor(_wrap("t", t3body))
        t_b3 = C["t"]

        def v5body(v):
            w(v, "t", t_b3)
            v.tensor_copy(bout_bc[:, 0:512], pb[0][:])
            v.tensor_copy(bout_bc[:, 512:1024], pb[1][:])
            inc(LAST["v"], "v")
        blk.vector(_wrap("v", v5body))
        set_free(("accrdy", 0), "t")
        v_bias = C["v"]

        kin0 = wqkvg_h.rearrange("(fc p) c -> p fc c", p=128)
        wo0 = wout_h.rearrange("(ec p) f -> p ec f", p=128)
        slab_src = []
        for g in range(4):
            slab_src.append([
                (512, 0, 256, kin0[:, :, 1024 + g * 256:1024 + (g + 1) * 256]),
                (512, 256, 512, kin0[:, :, 2048 + g * 256:2048 + (g + 1) * 256]),
            ])
        for piece in range(2):
            slab_src.append([(1024, 0, 1024,
                              kin0[:, piece * 4:(piece + 1) * 4, 0:1024])])
        for piece in range(2):
            slab_src.append([(1024, 0, 1024,
                              wo0[:, piece * 4:(piece + 1) * 4, :])])
        for e in range(8):
            we = wexp_h[e].rearrange("(hc p) f -> p hc f", p=128)
            for fs in range(2):
                slab_src.append([(512, 0, 512,
                                  we[:, :, fs * 512:(fs + 1) * 512])])
        scr_ready = {}
        NSLAB = len(slab_src)
        assert NSLAB == 24

        def presplit_in(i, slot):
            stv = acc[:, (i % 2) * 4096:(i % 2) * 4096 + 4096]

            def gin(g, i=i, slot=slot, stv=stv):
                wait_free(g, ("pstg", i % 2))
                if i < 2 or i == 8:
                    w(g, "t", t_b3)
                w(g, "sw%d" % slot, C["sw%d" % slot])
                for (dsub, c0, c1, srcv) in slab_src[i]:
                    ov = stv.rearrange("p (fc d) -> p fc d", d=dsub)[:, :, c0:c1]
                    inc(g.dma_start(out=ov, in_=srcv), "sw%d" % slot, 16)
            blk.gpsimd(_wrap("g", gin))
            return C["sw%d" % slot]

        def presplit_out(i, slot, sw_in):
            stv = acc[:, (i % 2) * 4096:(i % 2) * 4096 + 4096]

            def gsp(g, i=i, slot=slot, stv=stv, sw_in=sw_in):
                w(g, "sw%d" % slot, sw_in)
                wait_free(g, ("whi", slot))
                wait_free(g, ("wlo", slot))
                g1 = inc(g.tensor_copy(whi[slot][:], stv), "g")
                w(g, "g", g1)
                g2 = inc(g.tensor_sub(wlo[slot][:], stv, whi[slot][:]), "g")
                w(g, "g", g2)
                w(g, "sq%d" % (i % 2), C["sq%d" % (i % 2)])
                inc(g.dma_start(out=scr_hi[i], in_=whi[slot][:]), "sq%d" % (i % 2), 16)
                inc(g.dma_start(out=scr_lo[i], in_=wlo[slot][:]), "sq%d" % (i % 2), 16)
            blk.gpsimd(_wrap("g", gsp))
            scr_ready[i] = ("sq%d" % (i % 2), C["sq%d" % (i % 2)])
            set_free(("pstg", i % 2), "sq%d" % (i % 2))
            set_free(("whi", slot), "sq%d" % (i % 2))
            set_free(("wlo", slot), "sq%d" % (i % 2))

        _ps_inflight = []

        def presplit_unit(i, slot):
            sw_in = presplit_in(i, slot)
            _ps_inflight.append((i, slot, sw_in))
            if len(_ps_inflight) >= 2:
                presplit_out(*_ps_inflight.pop(0))

        def presplit_flush():
            while _ps_inflight:
                presplit_out(*_ps_inflight.pop(0))

        for i in range(2):
            presplit_unit(i, i % 2)
        presplit_pending = list(range(2, 24))
        free[("accrdy", 2)] = ("t", C["t"])

        accrdy_keys = {}


    def prep_units(blk, s, X, hooks_in_b=True, hooks_in_a=True):
        units = []

        def passA_chunk(c):
            cg = X * 8 + c

            def run():
                def syb(e, c=c):
                    wait_free(e, ("xbuf", c % 2))
                    inc(e.dma_start(out=xbuf[c % 2][:],
                                    in_=x_h[s, X * HALF + c * 128:X * HALF + (c + 1) * 128, :]),
                        "sx%d" % (c % 2), 16)
                blk.sync(syb)
                sxn = "sx%d" % (c % 2)
                sx_c = C[sxn]

                def ab(a, c=c, sxn=sxn, sx_c=sx_c):
                    w(a, sxn, sx_c)
                    wait_free(a, ("xnbuf", 0))
                    a.activation(out=xnbuf[c % 2][:], in_=xbuf[c % 2][:],
                                 func=AF.Square)
                    inc(LAST["a"], "a")
                blk.scalar(_wrap("a", ab))
                a_c = C["a"]

                def vb(v, c=c, sxn=sxn, sx_c=sx_c, a_c=a_c):
                    w(v, sxn, sx_c)
                    w(v, "a", a_c)
                    v.reduce_sum(sumt[:, 8 * c:8 * c + 1], xbuf[c % 2][:], axis=AX.X)
                    v.reduce_sum(sqt[:, 8 * c:8 * c + 1], xnbuf[c % 2][:], axis=AX.X)
                    inc(LAST["v"], "v")
                blk.vector(_wrap("v", vb))
                set_free(("xbuf", c % 2), "v")
                set_free(("xnbuf", 0), "v")
                if hooks_in_a and presplit_pending:
                    presplit_unit(presplit_pending.pop(0), 2)
                    if not presplit_pending:
                        presplit_flush()
                        free[("accrdy", 0)] = ("sq0", C["sq0"])
                        free[("accrdy", 1)] = ("sq1", C["sq1"])
            return run

        def stats_chain():
            def run():
                def vstats(v):
                    sv = lambda t_: t_[:].rearrange("p (c u) -> p c u", u=8)[:, :, 0:1]
                    mv = mean_t[:].rearrange("p (c u) -> p c u", u=8)[:, X * 8:X * 8 + 8, 0:1]
                    v.drain()
                    v.tensor_scalar_mul(mv, sv(sumt), 1.0 / E)
                    v.tensor_scalar_mul(sv(sq2t), sv(sqt), 1.0 / E)
                    v.drain()
                    v.tensor_mul(sv(m2t), mv, mv)
                    v.drain()
                    v.tensor_sub(sv(vart), sv(sq2t), sv(m2t))
                    v.drain()
                    inc(LAST["v"], "v")
                blk.vector(_wrap("v", vstats))
                v_var = C["v"]

                def astats(a):
                    w(a, "v", v_var)
                    a.activation(out=sstdt[:].rearrange("p (c u) -> p c u", u=8)[:, :, 0:1],
                                 in_=vart[:].rearrange("p (c u) -> p c u", u=8)[:, :, 0:1],
                                 func=AF.Sqrt, bias=eps5[:])
                    inc(LAST["a"], "a")
                blk.scalar(_wrap("a", astats))
                a_sstd = C["a"]

                def vrstd(v):
                    w(v, "a", a_sstd)
                    rv = rstd_t[:].rearrange("p (c u) -> p c u", u=8)[:, X * 8:X * 8 + 8, 0:1]
                    v.reciprocal(rv, sstdt[:].rearrange("p (c u) -> p c u", u=8)[:, :, 0:1])
                    v.drain()
                    inc(LAST["v"], "v")
                blk.vector(_wrap("v", vrstd))
            return run

        def passB_chunk(c):
            cg = X * 8 + c

            def run():
                def syb(e, c=c):
                    wait_free(e, ("xbuf", c % 2))
                    inc(e.dma_start(out=xbuf[c % 2][:],
                                    in_=x_h[s, X * HALF + c * 128:X * HALF + (c + 1) * 128, :]),
                        "sx%d" % (c % 2), 16)
                blk.sync(syb)
                sxn = "sx%d" % (c % 2)
                sx_c = C[sxn]

                def vb(v, c=c, cg=cg, sxn=sxn, sx_c=sx_c):
                    w(v, sxn, sx_c)
                    wait_free(v, ("xnbuf", 0))
                    v.tensor_scalar(xnbuf[c % 2][:], xbuf[c % 2][:],
                                    mean_t[:, 8 * cg:8 * cg + 1],
                                    rstd_t[:, 8 * cg:8 * cg + 1],
                                    ALU.subtract, ALU.mult)
                    inc(LAST["v"], "v")
                blk.vector(_wrap("v", vb))
                set_free(("xbuf", c % 2), "v")
                v_xn = C["v"]

                for b in range(2):
                    def tb(t, c=c, b=b, v_xn=v_xn):
                        w(t, "v", v_xn)
                        wait_free(t, ("bank", B_TR[b]))
                        for j in range(4):
                            fc = b * 4 + j
                            t.transpose(pb[B_TR[b]][:, j * 128:(j + 1) * 128],
                                        xnbuf[c % 2][:, fc * 128:(fc + 1) * 128],
                                        ident[:])
                        inc(LAST["t"], "t")
                    blk.tensor(_wrap("t", tb))
                    t_tr = C["t"]
                    if b == 1:
                        set_free(("xnbuf", 0), "t")

                    def ab(a, c=c, b=b, t_tr=t_tr):
                        w(a, "t", t_tr)
                        if c == 0 and b == 0:
                            wait_free(a, ("xnfm", X))
                        pv = pb[B_TR[b]][:].rearrange("p (j u) -> p j u", u=128)
                        ov = xnh[X][:].rearrange("p (fc u) -> p fc u", u=HALF)[
                            :, b * 4:(b + 1) * 4, c * 128:(c + 1) * 128]
                        a.activation(out=ov, in_=pv, func=AF.Copy)
                        inc(LAST["a"], "a")
                    blk.scalar(_wrap("a", ab))
                    a_hi = C["a"]

                    def vb2(v, c=c, b=b, t_tr=t_tr, a_hi=a_hi):
                        w(v, "t", t_tr)
                        w(v, "a", a_hi)
                        if c == 0 and b == 0:
                            wait_free(v, ("xnfm", X))
                        pv = pb[B_TR[b]][:].rearrange("p (j u) -> p j u", u=128)
                        hv = xnh[X][:].rearrange("p (fc u) -> p fc u", u=HALF)[
                            :, b * 4:(b + 1) * 4, c * 128:(c + 1) * 128]
                        lv = xnl[X][:].rearrange("p (fc u) -> p fc u", u=HALF)[
                            :, b * 4:(b + 1) * 4, c * 128:(c + 1) * 128]
                        v.tensor_tensor(lv, pv, hv, ALU.subtract)
                        inc(LAST["v"], "v")
                    blk.vector(_wrap("v", vb2))
                    set_free(("bank", B_TR[b]), "v")
                    if b == 1:
                        free[("xnrdy_a", X, c)] = ("a", C["a"])
                        free[("xnrdy_v", X, c)] = ("v", C["v"])
                if hooks_in_b and presplit_pending:
                    presplit_unit(presplit_pending.pop(0), 2)
                    if not presplit_pending:
                        presplit_flush()
                        free[("accrdy", 0)] = ("sq0", C["sq0"])
                        free[("accrdy", 1)] = ("sq1", C["sq1"])
            return run

        for c in range(NCH):
            units.append(passA_chunk(c))
        units.append(stats_chain())
        for c in range(NCH):
            units.append(passB_chunk(c))
        return units

    def accinit_units(blk, s, X):
        units = []

        def mk(c):
            cg = X * 8 + c

            def run():
                def syb(e, c=c):
                    wait_free(e, ("xbuf", c % 2))
                    inc(e.dma_start(out=xbuf[c % 2][:],
                                    in_=x_h[s, X * HALF + c * 128:X * HALF + (c + 1) * 128, :]),
                        "sx%d" % (c % 2), 16)
                blk.sync(syb)
                sxn = "sx%d" % (c % 2)
                sx_c = C[sxn]

                def vb(v, c=c, cg=cg, sxn=sxn, sx_c=sx_c):
                    w(v, sxn, sx_c)
                    wait_free(v, ("xnbuf", 0))
                    if c == 0:
                        v.drain()
                        wait_free(v, ("accrdy", 0))
                        wait_free(v, ("accrdy", 1))
                        wait_free(v, ("accrdy", 2))
                    v.tensor_scalar(xnbuf[c % 2][:], xbuf[c % 2][:],
                                    mean_t[:, 8 * cg:8 * cg + 1],
                                    rstd_t[:, 8 * cg:8 * cg + 1],
                                    ALU.subtract, ALU.mult)
                    v.drain()
                    v.tensor_add(acc[:, c * E:(c + 1) * E], xbuf[c % 2][:],
                                 xnbuf[c % 2][:])
                    inc(LAST["v"], "v")
                blk.vector(_wrap("v", vb))
                set_free(("xbuf", c % 2), "v")
                set_free(("xnbuf", 0), "v")
                if presplit_pending:
                    presplit_unit(presplit_pending.pop(0), 2)
                    if not presplit_pending:
                        presplit_flush()
                        free[("accrdy", 0)] = ("sq0", C["sq0"])
                        free[("accrdy", 1)] = ("sq1", C["sq1"])
            return run

        for c in range(NCH):
            units.append(mk(c))
        return units

    def stream_pair(blk, slot, i):
        swn = "sw%d" % slot

        def gb(g):
            wait_free(g, ("whi", slot))
            wait_free(g, ("wlo", slot))
            nm, val = scr_ready[i]
            w(g, nm, val)
            inc(g.dma_start(out=whi[slot][:], in_=scr_hi[i]), swn, 16)
            inc(g.dma_start(out=wlo[slot][:], in_=scr_lo[i]), swn, 16)
        blk.gpsimd(_wrap("g", gb))
        return swn, C[swn]

    def mm3(t, bank_ap, rhs_cols, xcol, first, last, sl):
        passes = [(xnh[sl], 0), (xnh[sl], 1), (xnl[sl], 0)]
        n = 0
        for (xs_, wi) in passes:
            for fc in range(8):
                t.matmul(bank_ap,
                         xs_[:, fc * HALF + xcol:fc * HALF + xcol + 128],
                         rhs_cols(fc, wi),
                         start=(first and n == 0), stop=(last and n == 23))
                n += 1

    def emit_kv(blk, s, X, side=None):
        side = list(side) if side else []
        def proj_unit(g, c, slot, wnm, wcnt):
            pk = B_PRJ[c % 2]

            def tb(t):
                w(t, wnm, wcnt)
                wait_free(t, ("xnrdy_a", X, c))
                wait_free(t, ("xnrdy_v", X, c))
                wait_free(t, ("bank", pk))
                wp = [whi[slot], wlo[slot]]
                mm3(t, pb[pk][:],
                    lambda fc, wi, wp=wp: wp[wi][:, fc * 512:(fc + 1) * 512],
                    c * 128, True, True, X)
                inc(LAST["t"], "t")
            blk.tensor(_wrap("t", tb))
            return C["t"], pk

        def evac_unit(g, c, pk, t_kvp):
            def vb(v):
                w(v, "t", t_kvp)
                wait_free(v, ("ksb", c % 2))
                wait_free(v, ("vsb", c % 2))
                v.tensor_tensor(k_sb[c % 2][:], pb[pk][:, 0:256],
                                bkv_bc[:, g * 256:(g + 1) * 256], ALU.add)
                pv = pb[pk][:].rearrange("p (h d) -> p h d", d=64)[:, 4:8, :]
                bv = bkv_bc[:].rearrange("p (h d) -> p h d", d=64)[
                    :, 16 + g * 4:16 + (g + 1) * 4, :]
                ov = v_sb[c % 2][:].rearrange("p (h d) -> p h d", d=65)[:, :, 0:64]
                v.tensor_tensor(ov, pv, bv, ALU.add)
                inc(LAST["v"], "v")
            blk.vector(_wrap("v", vb))
            set_free(("bank", pk), "v")
            return C["v"]

        def kcs_unit(g, c, cg, v_evac):
            def ab(a):
                w(a, "v", v_evac)
                wait_free(a, ("kcsa", c % 2))
                iv = k_sb[c % 2][:].rearrange("p (h d) -> p h d", d=64)
                oc = kcs_all[c % 2][:].rearrange("p (h d) -> p h d", d=128)[
                    :, :, 0:64]
                os_ = kcs_all[c % 2][:].rearrange("p (h d) -> p h d", d=128)[
                    :, :, 64:128]
                a.activation(out=oc, in_=iv, func=AF.Relu,
                             scale=coscols[:, cg:cg + 1])
                a.activation(out=os_, in_=iv, func=AF.Relu,
                             scale=sincols[:, cg:cg + 1])
                inc(LAST["a"], "a")
            blk.scalar(_wrap("a", ab))
            return C["a"]

        def kvmm_unit(g, c, cg, X, a_kcs):
            def tb2(t):
                w(t, "a", a_kcs)
                for hl in range(4):
                    h = g * 4 + hl
                    if h < 7:
                        bk, col, hlast = B_KV[0], h * 65, 6
                    elif h < 14:
                        bk, col, hlast = B_KV[1], (h - 7) * 65, 13
                    else:
                        bk, col, hlast = B_KV[2], (h - 14) * 65, 15
                    first = not kv_started.get((s, bk), False)
                    kv_started[(s, bk)] = True
                    if first:
                        wait_free(t, ("bank", bk))
                    last = (X == 1 and c == NCH - 1 and h == hlast)
                    t.matmul(pb[bk][:, col:col + 65],
                             kcs_all[c % 2][:, hl * 128:(hl + 1) * 128],
                             v_sb[c % 2][:, hl * 65:hl * 65 + 65],
                             start=first, stop=last)
                inc(LAST["t"], "t")
            blk.tensor(_wrap("t", tb2))
            set_free(("kcsa", c % 2), "t")
            set_free(("ksb", c % 2), "t")
            set_free(("vsb", c % 2), "t")

        def gate_unit(g, c, cg):
            def tg(t):
                wait_free(t, ("bank", B_G))
                gp = [ghi, glo]
                n = 0
                for (xs_, wi) in [(xnh[X], 0), (xnh[X], 1), (xnl[X], 0)]:
                    for fc in range(8):
                        t.matmul(pb[B_G][:, 0:8],
                                 xs_[:, fc * HALF + c * 128:fc * HALF + c * 128 + 128],
                                 gp[wi][:, fc * 8:(fc + 1) * 8],
                                 start=(n == 0), stop=(n == 23))
                        n += 1
                inc(LAST["t"], "t")
            blk.tensor(_wrap("t", tg))
            t_g = C["t"]

            def vg(v):
                w(v, "t", t_g)
                v.reduce_max(sstdt[:, 8 * c:8 * c + 1], pb[B_G][:, 0:8], axis=AX.X)
                v.drain()
                v.tensor_scalar(masks[:, cg * 8:cg * 8 + 8], pb[B_G][:, 0:8],
                                sstdt[:, 8 * c:8 * c + 1], None, ALU.is_equal)
                inc(LAST["v"], "v")
            blk.vector(_wrap("v", vg))
            set_free(("bank", B_G), "v")

        for g in range(4):
            slot = g % 2
            wnm, wcnt = stream_pair(blk, slot, g)

            pend = None
            for c in range(NCH):
                cg = X * 8 + c
                for _ in range(2):
                    if side:
                        side.pop(0)()
                t_kvp, pk = proj_unit(g, c, slot, wnm, wcnt)
                if c >= 6:
                    set_free(("whi", slot), "t")
                    set_free(("wlo", slot), "t")
                if pend is not None:
                    pc, pcg, pa = pend
                    kvmm_unit(g, pc, pcg, X, pa)
                    if g == 3:
                        gate_unit(g, pc, pcg)
                v_evac = evac_unit(g, c, pk, t_kvp)
                a_kcs = kcs_unit(g, c, cg, v_evac)
                pend = (c, cg, a_kcs)
            pc, pcg, pa = pend
            kvmm_unit(g, pc, pcg, X, pa)
            if g == 3:
                gate_unit(g, pc, pcg)
        while side:
            side.pop(0)()

    def emit_kv_end():
        set_free(("xnfm", 0), "t")

    def emit_kvn_evac(blk, s):
        def vb(v):
            w(v, "t", C["t"])
            wait_free(v, ("kvnused", 0))
            v.tensor_copy(kvn[:, 0:455], pb[B_KV[0]][:, 0:455])
            v.tensor_copy(kvn[:, 455:910], pb[B_KV[1]][:, 0:455])
            v.tensor_copy(kvn[:, 910:1040], pb[B_KV[2]][:, 0:130])
            inc(LAST["v"], "v")
        blk.vector(_wrap("v", vb))
        v_kvn = C["v"]
        for bk in B_KV:
            kv_started[(s, bk)] = False
            set_free(("bank", bk), "v")

        def gb(g):
            w(g, "v", v_kvn)
            wait_free(g, ("kvnused", 0))
            inc(g.dma_start(out=kvn_sw[0:64, :], in_=kvn[64:128, :]), "skv", 16)
            inc(g.dma_start(out=kvn_sw[64:128, :], in_=kvn[0:64, :]), "skv", 16)
        blk.gpsimd(_wrap("g", gb))
        set_free(("kvnrdy", 0), "skv")

    B_Q = (6, 7)

    def emit_p2compute(blk, s, X, side=None, side_pre=None):
        for u in (side_pre or []):
            u()
        side = list(side) if side else []
        """Per window: a batch of expert slabs, then the attention window with
        Wq/Wout streamed through the same 3-slot slab ring (the PE is busy
        with q/Wout matmuls while those slots are occupied)."""
        ring = {"pos": 0}

        def next_slot():
            slotv = ring["pos"] % 3
            ring["pos"] += 1
            return slotv

        def exp_slab_units(j):
            e, fs = j // 2, j % 2
            slot = next_slot()
            wnm, wcnt = stream_pair(blk, slot, 8 + j)
            for c in range(NCH):
                cg = X * 8 + c
                pk = B_PRJ[c % 2]

                def tb(t, c=c, pk=pk, slot=slot, wnm=wnm, wcnt=wcnt):
                    w(t, wnm, wcnt)
                    wait_free(t, ("bank", pk))
                    wp = [whi[slot], wlo[slot]]
                    mm3(t, pb[pk][:],
                        lambda fc, wi, wp=wp: wp[wi][:, fc * 512:(fc + 1) * 512],
                        c * 128, True, True, X)
                    inc(LAST["t"], "t")
                blk.tensor(_wrap("t", tb))
                t_y = C["t"]
                if c >= 6:
                    set_free(("whi", slot), "t")
                    set_free(("wlo", slot), "t")

                def vb(v, e=e, fs=fs, c=c, cg=cg, pk=pk, t_y=t_y):
                    w(v, "t", t_y)
                    sl = acc[:, c * E + fs * 512:c * E + (fs + 1) * 512]
                    v.scalar_tensor_tensor(sl, pb[pk][:],
                                           masks[:, cg * 8 + e:cg * 8 + e + 1],
                                           sl, op0=ALU.mult, op1=ALU.add)
                    if c == NCH - 1:
                        v.drain()
                    inc(LAST["v"], "v")
                blk.vector(_wrap("v", vb))
                set_free(("bank", pk), "v")

        for wdx in range(4):
            gw = X * 4 + wdx
            nside = len(side) if wdx == 3 else (len(side) + 3 - wdx) // (4 - wdx)
            for _ in range(nside):
                side.pop(0)()
            for j in range(4 * wdx, 4 * wdx + 4):
                exp_slab_units(j)

            sq0, sq1 = next_slot(), next_slot()
            qn0, qv0 = stream_pair(blk, sq0, 4)
            qn1, qv1 = stream_pair(blk, sq1, 5)
            whiQ = [whi[sq0], whi[sq1]]
            wloQ = [wlo[sq0], wlo[sq1]]

            def strig(e, gw=gw):
                wait_free(e, ("trig", 0))
                off = gw * 256
                inc(e.dma_start(out=coswin[:],
                                in_=trig_h[0, off:off + 256].rearrange(
                                    "(a b) -> a b", a=1).partition_broadcast(128)),
                    "str", 16)
                inc(e.dma_start(out=sinwin[:],
                                in_=trig_h[1, off:off + 256].rearrange(
                                    "(a b) -> a b", a=1).partition_broadcast(128)),
                    "str", 16)
            blk.sync(strig)
            s_trig = C["str"]

            v_atp_d = {}

            def tail_ctx(qc, v_qcs):
                for hh in range(2):
                    h = 2 * qc + hh
                    cb = B_KV[hh]

                    def tc(t, qc=qc, hh=hh, h=h, cb=cb, v_qcs=v_qcs):
                        w(t, "v", v_qcs)
                        wait_free(t, ("bank", cb))
                        wait_free(t, ("kvnrdy", 0))
                        kc = kvn if hh == 0 else kvn_sw
                        ks = kvn_sw if hh == 0 else kvn
                        b0 = hh * 64
                        for ts in range(2):
                            t.matmul(pb[cb][0:128, ts * 65:ts * 65 + 65],
                                     qcos[qc % 2][b0:b0 + 64,
                                                  ts * 128:(ts + 1) * 128],
                                     kc[b0:b0 + 64, h * 65:h * 65 + 65],
                                     start=True, stop=False)
                            t.matmul(pb[cb][0:128, ts * 65:ts * 65 + 65],
                                     qsin[qc % 2][b0:b0 + 64,
                                                  ts * 128:(ts + 1) * 128],
                                     ks[b0:b0 + 64, h * 65:h * 65 + 65],
                                     start=False, stop=True)
                        inc(LAST["t"], "t")
                    blk.tensor(_wrap("t", tc))
                    t_ctx = C["t"]
                    if hh == 1:
                        set_free(("qcs", qc % 2), "t")

                    def vc(v, hh=hh, cb=cb, t_ctx=t_ctx):
                        w(v, "t", t_ctx)
                        pv = pb[cb][:, 0:130].rearrange("p (ts u) -> p ts u", u=65)[
                            :, :, 64:65]
                        nv = nrm4[:, hh * 4:hh * 4 + 4].rearrange(
                            "p (a u) -> p a u", u=2)[:, :, 0:1]
                        v.tensor_scalar_add(nv, pv, AT_EPS)
                        v.drain()
                        v.reciprocal(rcp4[:, hh * 4:hh * 4 + 4].rearrange(
                            "p (a u) -> p a u", u=2)[:, :, 0:1], nv)
                        v.drain()
                        inc(LAST["v"], "v")
                    blk.vector(_wrap("v", vc))

                    for ts in range(2):
                        def va(v, qc=qc, cb=cb, hh=hh, ts=ts):
                            if hh == 0:
                                wait_free(v, ("atile", qc % 2, ts))
                            v.tensor_scalar_mul(
                                atile[qc % 2][ts][:, hh * 64:hh * 64 + 64],
                                pb[cb][:, ts * 65:ts * 65 + 64],
                                rcp4[:, hh * 4 + ts * 2:hh * 4 + ts * 2 + 1])
                            inc(LAST["v"], "v")
                        blk.vector(_wrap("v", va))
                    free[("bank", cb)] = ("v", C["v"])
                v_atp_d[qc] = C["v"]

            def tail_atp(qc):
                v_atp = v_atp_d[qc]
                patp = B_TR[qc % 2]
                for ts in range(2):
                    def ta(t, qc=qc, ts=ts, patp=patp, v_atp=v_atp):
                        w(t, "v", v_atp)
                        if ts == 0:
                            wait_free(t, ("bank", patp))
                        t.transpose(pb[patp][0:128, ts * 128:(ts + 1) * 128],
                                    atile[qc % 2][ts][:], ident[:])
                        inc(LAST["t"], "t")
                    blk.tensor(_wrap("t", ta))
                    set_free(("atile", qc % 2, ts), "t")
                t_atp = C["t"]

                def aatp(a, qc=qc, patp=patp, t_atp=t_atp):
                    w(a, "t", t_atp)
                    wait_free(a, ("afm", qc))
                    a.activation(out=afm_hi[:, qc * 256:(qc + 1) * 256],
                                 in_=pb[patp][:, 0:256], func=AF.Copy)
                    inc(LAST["a"], "a")
                blk.scalar(_wrap("a", aatp))
                a_afm = C["a"]

                def vatp(v, qc=qc, patp=patp, t_atp=t_atp, a_afm=a_afm):
                    w(v, "t", t_atp)
                    w(v, "a", a_afm)
                    v.tensor_sub(afm_lo[:, qc * 256:(qc + 1) * 256],
                                 pb[patp][:, 0:256],
                                 afm_hi[:, qc * 256:(qc + 1) * 256])
                    inc(LAST["v"], "v")
                blk.vector(_wrap("v", vatp))
                set_free(("bank", patp), "v")
                free[("attnfm_rdy", qc)] = ("v", C["v"])

            pend_ctx = None
            pend_atp = None
            for qc in range(8):
                pq = B_Q[qc % 2]

                def tq(t, qc=qc, wdx=wdx, pq=pq):
                    w(t, qn0, qv0)
                    w(t, qn1, qv1)
                    wait_free(t, ("bank", pq))
                    n = 0
                    for (xi, wi) in [(0, 0), (0, 1), (1, 0)]:
                        for fc in range(8):
                            wp = [whiQ, wloQ][wi][fc // 4]
                            lhsT = wp[:, (fc % 4) * 1024 + qc * 128:
                                      (fc % 4) * 1024 + (qc + 1) * 128]
                            xs_ = [xnh[X], xnl[X]][xi]
                            rr = xs_[:, fc * HALF + wdx * 256:fc * HALF + (wdx + 1) * 256]
                            t.matmul(pb[pq][:, 0:256], lhsT, rr,
                                     start=(n == 0), stop=(n == 23))
                            n += 1
                    inc(LAST["t"], "t")
                blk.tensor(_wrap("t", tq))
                t_qp = C["t"]
                if qc == 7:
                    set_free(("whi", sq0), "t")
                    set_free(("wlo", sq0), "t")
                    set_free(("whi", sq1), "t")
                    set_free(("wlo", sq1), "t")

                def aq2(a, qc=qc, pq=pq, t_qp=t_qp):
                    w(a, "t", t_qp)
                    wait_free(a, ("qrelu", qc % 2))
                    a.activation(out=qrelu[qc % 2][:], in_=pb[pq][:, 0:256],
                                 func=AF.Relu, bias=bq_cols[:, qc:qc + 1])
                    inc(LAST["a"], "a")
                blk.scalar(_wrap("a", aq2))
                a_qr = C["a"]
                set_free(("bank", pq), "a")

                def vq2(v, qc=qc, a_qr=a_qr, s_trig=s_trig):
                    w(v, "a", a_qr)
                    w(v, "str", s_trig)
                    wait_free(v, ("qcs", qc % 2))
                    v.tensor_mul(qcos[qc % 2][:], qrelu[qc % 2][:], coswin[:])
                    v.tensor_mul(qsin[qc % 2][:], qrelu[qc % 2][:], sinwin[:])
                    inc(LAST["v"], "v")
                blk.vector(_wrap("v", vq2))
                v_qcs = C["v"]
                set_free(("qrelu", qc % 2), "v")
                if qc == 7:
                    set_free(("trig", 0), "v")

                if pend_ctx is not None:
                    tail_ctx(*pend_ctx)
                if pend_atp is not None:
                    tail_atp(pend_atp)
                pend_atp = pend_ctx[0] if pend_ctx is not None else None
                pend_ctx = (qc, v_qcs)
            tail_ctx(*pend_ctx)
            if pend_atp is not None:
                tail_atp(pend_atp)
            tail_atp(pend_ctx[0])

            so0_, so1_ = next_slot(), next_slot()
            on0, ov0 = stream_pair(blk, so0_, 6)
            on1, ov1 = stream_pair(blk, so1_, 7)
            whiO = [whi[so0_], whi[so1_]]
            wloO = [wlo[so0_], wlo[so1_]]
            for ts in range(2):
                for of in range(2):
                    pw = B_TR[(ts * 2 + of) % 2]

                    def tw(t, wdx=wdx, ts=ts, of=of, pw=pw):
                        w(t, on0, ov0)
                        w(t, on1, ov1)
                        for qc in range(8):
                            nm, val = free[("attnfm_rdy", qc)]
                            w(t, nm, val)
                        wait_free(t, ("bank", pw))
                        n = 0
                        for (xi, wi) in [(0, 0), (0, 1), (1, 0)]:
                            for ec in range(8):
                                af = [afm_hi, afm_lo][xi]
                                wp = [whiO, wloO][wi][ec // 4]
                                t.matmul(pb[pw][:],
                                         af[:, ec * 256 + ts * 128:ec * 256 + (ts + 1) * 128],
                                         wp[:, (ec % 4) * 1024 + of * 512:
                                            (ec % 4) * 1024 + (of + 1) * 512],
                                         start=(n == 0), stop=(n == 23))
                                n += 1
                        inc(LAST["t"], "t")
                    blk.tensor(_wrap("t", tw))
                    t_wo = C["t"]
                    if ts == 1 and of == 1:
                        set_free(("whi", so0_), "t")
                        set_free(("wlo", so0_), "t")
                        set_free(("whi", so1_), "t")
                        set_free(("wlo", so1_), "t")

                    def vw(v, wdx=wdx, ts=ts, of=of, pw=pw, t_wo=t_wo):
                        w(v, "t", t_wo)
                        sl = acc[:, (wdx * 2 + ts) * E + of * 512:
                                 (wdx * 2 + ts) * E + (of + 1) * 512]
                        v.tensor_add(sl, sl, pb[pw][:])
                        if ts == 1 and of == 1:
                            v.drain()
                        inc(LAST["v"], "v")
                    blk.vector(_wrap("v", vw))
                    set_free(("bank", pw), "v")
            for qc in range(8):
                free[("afm", qc)] = ("t", C["t"])
        set_free(("xnfm", X), "t")
        set_free(("kvnused", 0), "t")

    def out_units(blk, s, X):
        units = []

        def mk(c):
            def run():
                def vb(v, c=c):
                    wait_free(v, ("xbuf", c % 2))
                    if c == 0:
                        v.drain()
                    v.tensor_add(out_sb[c % 2][:], acc[:, c * E:(c + 1) * E],
                                 bout_bc[:])
                    inc(LAST["v"], "v")
                blk.vector(_wrap("v", vb))
                v_o = C["v"]

                def syb(e, c=c, v_o=v_o):
                    w(e, "v", v_o)
                    inc(e.dma_start(
                        out=out_h[s, X * HALF + c * 128:X * HALF + (c + 1) * 128, :],
                        in_=out_sb[c % 2][:]),
                        "so%d" % (c % 2), 16)
                blk.sync(syb)
                set_free(("xbuf", c % 2), "so%d" % (c % 2))
            return run

        for c in range(NCH):
            units.append(mk(c))
        return units

    def emit_out(blk, s, X):
        for u in out_units(blk, s, X):
            u()

    uA_next = None
    for s in range(2):
        uA = prep_units(blk, s, 0, hooks_in_b=False) if uA_next is None else uA_next
        if s == 0:
            for u in uA[:NCH]:
                u()
        uB = prep_units(blk, s, 1)
        emit_kv(blk, s, 0, side=(uA[NCH:] + uB) if s == 0 else uB)
        while presplit_pending:
            presplit_unit(presplit_pending.pop(0), 2)
        presplit_flush()
        if ("accrdy", 0) not in free:
            free[("accrdy", 0)] = ("sq0", C["sq0"])
            free[("accrdy", 1)] = ("sq1", C["sq1"])
        ai0 = accinit_units(blk, s, 0)
        emit_kv(blk, s, 1, side=ai0)
        emit_kvn_evac(blk, s)
        side1 = accinit_units(blk, s, 1)
        if s == 0:
            uA_next = prep_units(blk, s + 1, 0)
        emit_p2compute(blk, s, 0, side=(uA_next[:9] if s == 0 else None))
        emit_p2compute(blk, s, 1, side=(uA_next[9:] if s == 0 else None),
                       side_pre=out_units(blk, s, 0) + side1)
        if s == 0:
            uA_next = []
        emit_out(blk, s, 1)

    def syfin(e):
        w(e, "so0", C["so0"])
        w(e, "so1", C["so1"])
    blk.sync(syfin)

    blk_cm.__exit__(None, None, None)
    st.close()
    return nc


_NC_CACHE = None


def kernel(L0_attn_x, L0_attn_W_qkv, L0_attn_b_qkv, L0_attn_W_out, L0_attn_b_out,
           L0_ffn_gate_w, L0_ffn_expert_w):
    global _NC_CACHE
    x = np.ascontiguousarray(np.asarray(L0_attn_x, dtype=np.float32))
    wqkv = np.asarray(L0_attn_W_qkv, dtype=np.float32)
    gate = np.asarray(L0_ffn_gate_w, dtype=np.float32)
    wqkvg = np.ascontiguousarray(np.concatenate([wqkv, gate.T], axis=1))
    wout = np.ascontiguousarray(np.asarray(L0_attn_W_out, dtype=np.float32))
    bqkv = np.ascontiguousarray(np.asarray(L0_attn_b_qkv, dtype=np.float32))
    bout = np.ascontiguousarray(np.asarray(L0_attn_b_out, dtype=np.float32))
    wexp = np.ascontiguousarray(
        np.asarray(L0_ffn_expert_w, dtype=np.float32).transpose(0, 2, 1))
    with jax.default_device(jax.local_devices(backend="cpu")[0]):
        positions = jnp.arange(L, dtype=jnp.float32)
        trig = np.ascontiguousarray(np.stack([
            np.asarray(jnp.cos(jnp.pi / 2 * positions / L), dtype=np.float32),
            np.asarray(jnp.sin(jnp.pi / 2 * positions / L), dtype=np.float32)]))

    if _NC_CACHE is None:
        _NC_CACHE = build()
    nc = _NC_CACHE

    in_maps = []
    for i in range(8):
        in_maps.append({
            "x": x[2 * i:2 * i + 2],
            "wqkvg": wqkvg,
            "wout": wout,
            "bqkv": bqkv,
            "bout": bout,
            "wexp": wexp,
            "trig": trig,
        })
    res = run_bass_kernel_spmd(nc, in_maps, core_ids=list(range(8)))
    out = np.concatenate([r["out"] for r in res.results], axis=0)
    return out.reshape(16, 2048, 1024)


# version 19
# speedup vs baseline: 1.0584x; 1.0584x over previous
import contextlib
import numpy as np
import jax
import jax.numpy as jnp
import concourse.bass as bass
import concourse.mybir as mybir
from concourse.bass_utils import run_bass_kernel_spmd

F32 = mybir.dt.float32
BF16 = mybir.dt.bfloat16
AF = mybir.ActivationFunctionType
ALU = mybir.AluOpType
AX = mybir.AxisListType

PI = float(np.pi)
C_HI = float(np.float32(np.pi / 4096.0))
C_LO = float(np.float32(np.pi / 4096.0 - C_HI))
PH_HI = float(np.float32(np.pi / 2.0))
PH_LO = float(np.float32(np.pi / 2.0 - PH_HI))
E = 1024
NH = 16
HD = 64
L = 2048
HALF = 1024
NCH = 8
LN_EPS = 1e-5
AT_EPS = 1e-4


def build():
    nc = bass.Bass("TRN2", target_bir_lowering=False)

    x_h = nc.dram_tensor("x", [2, L, E], F32, kind="ExternalInput").ap()
    wqkvg_h = nc.dram_tensor("wqkvg", [E, 3080], F32, kind="ExternalInput").ap()
    wout_h = nc.dram_tensor("wout", [E, E], F32, kind="ExternalInput").ap()
    bqkv_h = nc.dram_tensor("bqkv", [3072], F32, kind="ExternalInput").ap()
    bout_h = nc.dram_tensor("bout", [E], F32, kind="ExternalInput").ap()
    wexp_h = nc.dram_tensor("wexp", [8, E, E], F32, kind="ExternalInput").ap()
    trig_h = nc.dram_tensor("trig", [2, L], F32, kind="ExternalInput").ap()
    out_h = nc.dram_tensor("out", [2, L, E], F32, kind="ExternalOutput").ap()
    scr_hi = nc.dram_tensor("scr_hi", [24, 128, 4096], BF16, kind="Internal").ap()
    scr_lo = nc.dram_tensor("scr_lo", [24, 128, 4096], BF16, kind="Internal").ap()

    st = contextlib.ExitStack()

    def sb(name, shape, dt=F32):
        return st.enter_context(nc.sbuf_tensor(name, shape, dt))

    xbuf = [sb(f"xbuf{i}", [128, E]) for i in range(2)]
    xnbuf = [sb("xnbuf0", [128, E])]
    xnbuf.append(xnbuf[0])
    out_sb = xbuf
    acc = sb("acc", [128, NCH * E])
    xnh = [sb(f"xnh{i}", [128, 8 * HALF], BF16) for i in range(2)]
    xnl = [sb(f"xnl{i}", [128, 8 * HALF], BF16) for i in range(2)]
    afm_hi = sb("afm_hi", [128, 8 * 256], BF16)
    afm_lo = sb("afm_lo", [128, 8 * 256], BF16)

    whi = [sb(f"whi{i}", [128, 4096], BF16) for i in range(3)]
    wlo = [sb(f"wlo{i}", [128, 4096], BF16) for i in range(3)]

    ghi = sb("ghi", [128, 64], BF16)
    glo = sb("glo", [128, 64], BF16)

    k_sb = [sb(f"ksb{i}", [128, 256]) for i in range(2)]
    v_sb = [sb(f"vsb{i}", [128, 4 * 65]) for i in range(2)]
    kcs_all = [sb(f"kcsa{i}", [128, 512]) for i in range(2)]
    gsb = kcs_all[0]
    kvn = sb("kvn", [128, NH * 65])
    kvn_sw = sb("kvn_sw", [128, NH * 65])

    qrelu = [sb(f"qrelu{i}", [128, 256]) for i in range(2)]
    qcos = [sb(f"qcos{i}", [128, 256]) for i in range(2)]
    qsin = [sb(f"qsin{i}", [128, 256]) for i in range(2)]
    atile = [[sb(f"atile{i}_{j}", [128, 128]) for j in range(2)] for i in range(2)]
    nrm4 = sb("nrm4", [128, 8])
    rcp4 = sb("rcp4", [128, 8])

    sumt = sb("sumt", [128, 64])
    sqt = sb("sqt", [128, 64])
    sq2t = sb("sq2t", [128, 64])
    vart = sb("vart", [128, 64])
    m2t = sumt
    sstdt = sb("sstdt", [128, 64])
    mean_t = sb("mean_t", [128, 128])
    rstd_t = sb("rstd_t", [128, 128])
    masks = sb("masks", [128, 128])

    ident = sb("ident", [128, 128])
    idp = atile[0][0]
    idf = atile[0][1]
    sincols = sb("sincols", [128, 16])
    coscols = sb("coscols", [128, 16])
    coswin = sb("coswin", [128, 256])
    sinwin = sb("sinwin", [128, 256])
    ones1 = sb("ones1", [1, 128])
    eps5 = sb("eps5", [128, 1])
    bq_cols = sb("bq_cols", [128, 8])
    bkv_bc = sb("bkv_bc", [128, 2048])
    bout_bc = sb("bout_bc", [128, 1024])

    pb = [st.enter_context(nc.psum_tensor(f"pb{i}", [128, 512], F32))
          for i in range(8)]
    B_TR = (0, 1)
    B_PRJ = (2, 3)
    B_KV = (4, 5, 6)
    B_G = 7

    S = {}
    C = {}
    for name in ["sx0", "sx1", "sw0", "sw1", "sw2", "so0", "so1", "sb", "skv",
                 "swq", "swo", "sq0", "sq1", "str", "t", "v", "a", "g"]:
        S[name] = st.enter_context(nc.semaphore(name))
        C[name] = 0

    def inc(instr, name, n=1):
        instr.then_inc(S[name], n)
        C[name] += n
        return C[name]

    def w(eng, name, val):
        if val > 0:
            eng.wait_ge(S[name], val)

    free = {}

    def wait_free(eng, key):
        if key in free:
            nm, val = free[key]
            w(eng, nm, val)

    def set_free(key, nm):
        free[key] = (nm, C[nm])

    kv_started = {}

    LAST = {}

    class _EP:
        def __init__(self, name, eng):
            self._n, self._e = name, eng

        def __getattr__(self, k):
            val = getattr(self._e, k)
            if not callable(val):
                return val

            def wrapped(*a, **kw):
                r = val(*a, **kw)
                if hasattr(r, "then_inc"):
                    LAST[self._n] = r
                return r
            return wrapped

    def _wrap(name, f):
        return lambda e, f=f, name=name: f(_EP(name, e))

    blk_cm = nc.Block()
    blk = blk_cm.__enter__()
    if True:
        def gbody(g):
            inc(g.iota(idp[:], [[0, 128]], channel_multiplier=1,
                       allow_small_or_imprecise_dtypes=True), "g")
            inc(g.iota(idf[:], [[1, 128]], channel_multiplier=0,
                       allow_small_or_imprecise_dtypes=True), "g")
            src = wqkvg_h.rearrange("(fc p) c -> p fc c", p=128)[:, :, 3072:3080]
            inc(g.dma_start(out=gsb[:, 0:64].rearrange("p (fc e) -> p fc e", e=8),
                            in_=src), "sb", 16)
            inc(g.dma_start(out=acc[0:1, 0:2048],
                            in_=bqkv_h[1024:3072].rearrange("(a b) -> a b", a=1)),
                "sb", 16)
            inc(g.dma_start(out=acc[0:1, 2048:3072],
                            in_=bout_h[:].rearrange("(a b) -> a b", a=1)),
                "sb", 16)
            inc(g.dma_start(out=bq_cols[:],
                            in_=bqkv_h[0:1024].rearrange("(c p) -> p c", p=128),
                            allow_slow_non_contiguous=True),
                "sb", 16)
            inc(g.dma_start(out=coscols[:],
                            in_=trig_h[0].rearrange("(c p) -> p c", p=128),
                            allow_slow_non_contiguous=True),
                "sb", 16)
            inc(g.dma_start(out=sincols[:],
                            in_=trig_h[1].rearrange("(c p) -> p c", p=128),
                            allow_slow_non_contiguous=True),
                "sb", 16)
        blk.gpsimd(_wrap("g", gbody))
        sw_setup = C["sb"]

        def vbody(v):
            w(v, "g", 2)
            v.tensor_tensor(ident[:], idp[:], idf[:], ALU.is_equal)
            v.memset(ones1[:], 1.0)
            v.memset(eps5[:], LN_EPS)
            v.memset(v_sb[0][:].rearrange("p (h c) -> p h c", c=65)[:, :, 64:65], 1.0)
            v.drain()
            v.memset(v_sb[1][:].rearrange("p (h c) -> p h c", c=65)[:, :, 64:65], 1.0)
            inc(LAST["v"], "v")
        blk.vector(_wrap("v", vbody))
        v_consts = C["v"]

        def abody(a):
            w(a, "v", v_consts)
            w(a, "sb", sw_setup)
            a.activation(out=ghi[:], in_=gsb[:, 0:64], func=AF.Copy)
            inc(LAST["a"], "a")
        blk.scalar(_wrap("a", abody))
        a_setup = C["a"]

        def v2body(v):
            w(v, "a", a_setup)
            v.tensor_sub(glo[:], gsb[:, 0:64], ghi[:])
            inc(LAST["v"], "v")
        blk.vector(_wrap("v", v2body))

        def tbody(t):
            w(t, "sb", sw_setup)
            w(t, "v", v_consts)
            for i in range(2):
                t.matmul(pb[i][:], ones1[:, 0:128], acc[0:1, i * 512:(i + 1) * 512],
                         start=True, stop=True)
            inc(LAST["t"], "t")
        blk.tensor(_wrap("t", tbody))
        t_b1 = C["t"]

        def v3body(v):
            w(v, "t", t_b1)
            v.tensor_copy(bkv_bc[:, 0:512], pb[0][:])
            v.tensor_copy(bkv_bc[:, 512:1024], pb[1][:])
            inc(LAST["v"], "v")
        blk.vector(_wrap("v", v3body))
        v_b1 = C["v"]

        def t2body(t):
            w(t, "v", v_b1)
            for i in range(2):
                t.matmul(pb[i][:], ones1[:, 0:128],
                         acc[0:1, 1024 + i * 512:1024 + (i + 1) * 512],
                         start=True, stop=True)
            inc(LAST["t"], "t")
        blk.tensor(_wrap("t", t2body))
        t_b2 = C["t"]

        def v4body(v):
            w(v, "t", t_b2)
            v.tensor_copy(bkv_bc[:, 1024:1536], pb[0][:])
            v.tensor_copy(bkv_bc[:, 1536:2048], pb[1][:])
            inc(LAST["v"], "v")
        blk.vector(_wrap("v", v4body))
        v_b2 = C["v"]

        def t3body(t):
            w(t, "v", v_b2)
            for i in range(2):
                t.matmul(pb[i][:], ones1[:, 0:128],
                         acc[0:1, 2048 + i * 512:2048 + (i + 1) * 512],
                         start=True, stop=True)
            inc(LAST["t"], "t")
        blk.tensor(_wrap("t", t3body))
        t_b3 = C["t"]

        def v5body(v):
            w(v, "t", t_b3)
            v.tensor_copy(bout_bc[:, 0:512], pb[0][:])
            v.tensor_copy(bout_bc[:, 512:1024], pb[1][:])
            inc(LAST["v"], "v")
        blk.vector(_wrap("v", v5body))
        set_free(("accrdy", 0), "t")
        v_bias = C["v"]

        kin0 = wqkvg_h.rearrange("(fc p) c -> p fc c", p=128)
        wo0 = wout_h.rearrange("(ec p) f -> p ec f", p=128)
        slab_src = []
        for g in range(4):
            slab_src.append([
                (512, 0, 256, kin0[:, :, 1024 + g * 256:1024 + (g + 1) * 256]),
                (512, 256, 512, kin0[:, :, 2048 + g * 256:2048 + (g + 1) * 256]),
            ])
        for piece in range(2):
            slab_src.append([(1024, 0, 1024,
                              kin0[:, piece * 4:(piece + 1) * 4, 0:1024])])
        for piece in range(2):
            slab_src.append([(1024, 0, 1024,
                              wo0[:, piece * 4:(piece + 1) * 4, :])])
        for e in range(8):
            we = wexp_h[e].rearrange("(hc p) f -> p hc f", p=128)
            for fs in range(2):
                slab_src.append([(512, 0, 512,
                                  we[:, :, fs * 512:(fs + 1) * 512])])
        scr_ready = {}
        NSLAB = len(slab_src)
        assert NSLAB == 24

        def presplit_in(i, slot):
            stv = acc[:, (i % 2) * 4096:(i % 2) * 4096 + 4096]

            def gin(g, i=i, slot=slot, stv=stv):
                wait_free(g, ("pstg", i % 2))
                if i < 2 or i == 8:
                    w(g, "t", t_b3)
                w(g, "sw%d" % slot, C["sw%d" % slot])
                for (dsub, c0, c1, srcv) in slab_src[i]:
                    ov = stv.rearrange("p (fc d) -> p fc d", d=dsub)[:, :, c0:c1]
                    inc(g.dma_start(out=ov, in_=srcv), "sw%d" % slot, 16)
            blk.gpsimd(_wrap("g", gin))
            return C["sw%d" % slot]

        def presplit_out(i, slot, sw_in):
            stv = acc[:, (i % 2) * 4096:(i % 2) * 4096 + 4096]

            def asp(a, slot=slot, stv=stv, sw_in=sw_in):
                w(a, "sw%d" % slot, sw_in)
                wait_free(a, ("whi", slot))
                a.activation(out=whi[slot][:], in_=stv, func=AF.Copy)
                inc(LAST["a"], "a")
            blk.scalar(_wrap("a", asp))
            a_sp = C["a"]

            def vsp(v, slot=slot, stv=stv, sw_in=sw_in, a_sp=a_sp):
                w(v, "sw%d" % slot, sw_in)
                w(v, "a", a_sp)
                wait_free(v, ("wlo", slot))
                v.tensor_sub(wlo[slot][:], stv, whi[slot][:])
                inc(LAST["v"], "v")
            blk.vector(_wrap("v", vsp))
            v_sp = C["v"]

            def gout(g, i=i, slot=slot, a_sp=a_sp, v_sp=v_sp):
                w(g, "a", a_sp)
                w(g, "v", v_sp)
                w(g, "sq%d" % (i % 2), C["sq%d" % (i % 2)])
                inc(g.dma_start(out=scr_hi[i], in_=whi[slot][:]), "sq%d" % (i % 2), 16)
                inc(g.dma_start(out=scr_lo[i], in_=wlo[slot][:]), "sq%d" % (i % 2), 16)
            blk.gpsimd(_wrap("g", gout))
            scr_ready[i] = ("sq%d" % (i % 2), C["sq%d" % (i % 2)])
            set_free(("pstg", i % 2), "sq%d" % (i % 2))
            set_free(("whi", slot), "sq%d" % (i % 2))
            set_free(("wlo", slot), "sq%d" % (i % 2))

        _ps_inflight = []

        def presplit_unit(i, slot):
            sw_in = presplit_in(i, slot)
            _ps_inflight.append((i, slot, sw_in))
            if len(_ps_inflight) >= 2:
                presplit_out(*_ps_inflight.pop(0))

        def presplit_flush():
            while _ps_inflight:
                presplit_out(*_ps_inflight.pop(0))

        for i in range(2):
            presplit_unit(i, i % 2)
        presplit_pending = list(range(2, 24))
        free[("accrdy", 2)] = ("t", C["t"])

        accrdy_keys = {}


    def prep_units(blk, s, X, hooks_in_b=True, hooks_in_a=True):
        units = []

        def passA_chunk(c):
            cg = X * 8 + c

            def run():
                def syb(e, c=c):
                    wait_free(e, ("xbuf", c % 2))
                    inc(e.dma_start(out=xbuf[c % 2][:],
                                    in_=x_h[s, X * HALF + c * 128:X * HALF + (c + 1) * 128, :]),
                        "sx%d" % (c % 2), 16)
                blk.sync(syb)
                sxn = "sx%d" % (c % 2)
                sx_c = C[sxn]

                def ab(a, c=c, sxn=sxn, sx_c=sx_c):
                    w(a, sxn, sx_c)
                    wait_free(a, ("xnbuf", 0))
                    a.activation(out=xnbuf[c % 2][:], in_=xbuf[c % 2][:],
                                 func=AF.Square)
                    inc(LAST["a"], "a")
                blk.scalar(_wrap("a", ab))
                a_c = C["a"]

                def vb(v, c=c, sxn=sxn, sx_c=sx_c, a_c=a_c):
                    w(v, sxn, sx_c)
                    w(v, "a", a_c)
                    v.reduce_sum(sumt[:, 8 * c:8 * c + 1], xbuf[c % 2][:], axis=AX.X)
                    v.reduce_sum(sqt[:, 8 * c:8 * c + 1], xnbuf[c % 2][:], axis=AX.X)
                    inc(LAST["v"], "v")
                blk.vector(_wrap("v", vb))
                set_free(("xbuf", c % 2), "v")
                set_free(("xnbuf", 0), "v")
                if hooks_in_a and presplit_pending:
                    presplit_unit(presplit_pending.pop(0), 2)
                    if not presplit_pending:
                        presplit_flush()
                        free[("accrdy", 0)] = ("sq0", C["sq0"])
                        free[("accrdy", 1)] = ("sq1", C["sq1"])
            return run

        def stats_chain():
            def run():
                def vstats(v):
                    sv = lambda t_: t_[:].rearrange("p (c u) -> p c u", u=8)[:, :, 0:1]
                    mv = mean_t[:].rearrange("p (c u) -> p c u", u=8)[:, X * 8:X * 8 + 8, 0:1]
                    v.drain()
                    v.tensor_scalar_mul(mv, sv(sumt), 1.0 / E)
                    v.tensor_scalar_mul(sv(sq2t), sv(sqt), 1.0 / E)
                    v.drain()
                    v.tensor_mul(sv(m2t), mv, mv)
                    v.drain()
                    v.tensor_sub(sv(vart), sv(sq2t), sv(m2t))
                    v.drain()
                    inc(LAST["v"], "v")
                blk.vector(_wrap("v", vstats))
                v_var = C["v"]

                def astats(a):
                    w(a, "v", v_var)
                    a.activation(out=sstdt[:].rearrange("p (c u) -> p c u", u=8)[:, :, 0:1],
                                 in_=vart[:].rearrange("p (c u) -> p c u", u=8)[:, :, 0:1],
                                 func=AF.Sqrt, bias=eps5[:])
                    inc(LAST["a"], "a")
                blk.scalar(_wrap("a", astats))
                a_sstd = C["a"]

                def vrstd(v):
                    w(v, "a", a_sstd)
                    rv = rstd_t[:].rearrange("p (c u) -> p c u", u=8)[:, X * 8:X * 8 + 8, 0:1]
                    v.reciprocal(rv, sstdt[:].rearrange("p (c u) -> p c u", u=8)[:, :, 0:1])
                    v.drain()
                    inc(LAST["v"], "v")
                blk.vector(_wrap("v", vrstd))
            return run

        def passB_chunk(c):
            cg = X * 8 + c

            def run():
                def syb(e, c=c):
                    wait_free(e, ("xbuf", c % 2))
                    inc(e.dma_start(out=xbuf[c % 2][:],
                                    in_=x_h[s, X * HALF + c * 128:X * HALF + (c + 1) * 128, :]),
                        "sx%d" % (c % 2), 16)
                blk.sync(syb)
                sxn = "sx%d" % (c % 2)
                sx_c = C[sxn]

                def vb(v, c=c, cg=cg, sxn=sxn, sx_c=sx_c):
                    w(v, sxn, sx_c)
                    wait_free(v, ("xnbuf", 0))
                    v.tensor_scalar(xnbuf[c % 2][:], xbuf[c % 2][:],
                                    mean_t[:, 8 * cg:8 * cg + 1],
                                    rstd_t[:, 8 * cg:8 * cg + 1],
                                    ALU.subtract, ALU.mult)
                    inc(LAST["v"], "v")
                blk.vector(_wrap("v", vb))
                set_free(("xbuf", c % 2), "v")
                v_xn = C["v"]

                for b in range(2):
                    def tb(t, c=c, b=b, v_xn=v_xn):
                        w(t, "v", v_xn)
                        wait_free(t, ("bank", B_TR[b]))
                        for j in range(4):
                            fc = b * 4 + j
                            t.transpose(pb[B_TR[b]][:, j * 128:(j + 1) * 128],
                                        xnbuf[c % 2][:, fc * 128:(fc + 1) * 128],
                                        ident[:])
                        inc(LAST["t"], "t")
                    blk.tensor(_wrap("t", tb))
                    t_tr = C["t"]
                    if b == 1:
                        set_free(("xnbuf", 0), "t")

                    def ab(a, c=c, b=b, t_tr=t_tr):
                        w(a, "t", t_tr)
                        if c == 0 and b == 0:
                            wait_free(a, ("xnfm", X))
                        pv = pb[B_TR[b]][:].rearrange("p (j u) -> p j u", u=128)
                        ov = xnh[X][:].rearrange("p (fc u) -> p fc u", u=HALF)[
                            :, b * 4:(b + 1) * 4, c * 128:(c + 1) * 128]
                        a.activation(out=ov, in_=pv, func=AF.Copy)
                        inc(LAST["a"], "a")
                    blk.scalar(_wrap("a", ab))
                    a_hi = C["a"]

                    def vb2(v, c=c, b=b, t_tr=t_tr, a_hi=a_hi):
                        w(v, "t", t_tr)
                        w(v, "a", a_hi)
                        if c == 0 and b == 0:
                            wait_free(v, ("xnfm", X))
                        pv = pb[B_TR[b]][:].rearrange("p (j u) -> p j u", u=128)
                        hv = xnh[X][:].rearrange("p (fc u) -> p fc u", u=HALF)[
                            :, b * 4:(b + 1) * 4, c * 128:(c + 1) * 128]
                        lv = xnl[X][:].rearrange("p (fc u) -> p fc u", u=HALF)[
                            :, b * 4:(b + 1) * 4, c * 128:(c + 1) * 128]
                        v.tensor_tensor(lv, pv, hv, ALU.subtract)
                        inc(LAST["v"], "v")
                    blk.vector(_wrap("v", vb2))
                    set_free(("bank", B_TR[b]), "v")
                    if b == 1:
                        free[("xnrdy_a", X, c)] = ("a", C["a"])
                        free[("xnrdy_v", X, c)] = ("v", C["v"])
                if hooks_in_b and presplit_pending:
                    presplit_unit(presplit_pending.pop(0), 2)
                    if not presplit_pending:
                        presplit_flush()
                        free[("accrdy", 0)] = ("sq0", C["sq0"])
                        free[("accrdy", 1)] = ("sq1", C["sq1"])
            return run

        for c in range(NCH):
            units.append(passA_chunk(c))
        units.append(stats_chain())
        for c in range(NCH):
            units.append(passB_chunk(c))
        return units

    def accinit_units(blk, s, X):
        units = []

        def mk(c):
            cg = X * 8 + c

            def run():
                def syb(e, c=c):
                    wait_free(e, ("xbuf", c % 2))
                    inc(e.dma_start(out=xbuf[c % 2][:],
                                    in_=x_h[s, X * HALF + c * 128:X * HALF + (c + 1) * 128, :]),
                        "sx%d" % (c % 2), 16)
                blk.sync(syb)
                sxn = "sx%d" % (c % 2)
                sx_c = C[sxn]

                def vb(v, c=c, cg=cg, sxn=sxn, sx_c=sx_c):
                    w(v, sxn, sx_c)
                    wait_free(v, ("xnbuf", 0))
                    if c == 0:
                        v.drain()
                        wait_free(v, ("accrdy", 0))
                        wait_free(v, ("accrdy", 1))
                        wait_free(v, ("accrdy", 2))
                    v.tensor_scalar(xnbuf[c % 2][:], xbuf[c % 2][:],
                                    mean_t[:, 8 * cg:8 * cg + 1],
                                    rstd_t[:, 8 * cg:8 * cg + 1],
                                    ALU.subtract, ALU.mult)
                    v.drain()
                    v.tensor_add(acc[:, c * E:(c + 1) * E], xbuf[c % 2][:],
                                 xnbuf[c % 2][:])
                    inc(LAST["v"], "v")
                blk.vector(_wrap("v", vb))
                set_free(("xbuf", c % 2), "v")
                set_free(("xnbuf", 0), "v")
                if presplit_pending:
                    presplit_unit(presplit_pending.pop(0), 2)
                    if not presplit_pending:
                        presplit_flush()
                        free[("accrdy", 0)] = ("sq0", C["sq0"])
                        free[("accrdy", 1)] = ("sq1", C["sq1"])
            return run

        for c in range(NCH):
            units.append(mk(c))
        return units

    def stream_pair(blk, slot, i):
        swn = "sw%d" % slot

        def gb(g):
            wait_free(g, ("whi", slot))
            wait_free(g, ("wlo", slot))
            nm, val = scr_ready[i]
            w(g, nm, val)
            inc(g.dma_start(out=whi[slot][:], in_=scr_hi[i]), swn, 16)
            inc(g.dma_start(out=wlo[slot][:], in_=scr_lo[i]), swn, 16)
        blk.gpsimd(_wrap("g", gb))
        return swn, C[swn]

    def mm3(t, bank_ap, rhs_cols, xcol, first, last, sl):
        passes = [(xnh[sl], 0), (xnh[sl], 1), (xnl[sl], 0)]
        n = 0
        for (xs_, wi) in passes:
            for fc in range(8):
                t.matmul(bank_ap,
                         xs_[:, fc * HALF + xcol:fc * HALF + xcol + 128],
                         rhs_cols(fc, wi),
                         start=(first and n == 0), stop=(last and n == 23))
                n += 1

    def emit_kv(blk, s, X, side=None):
        side = list(side) if side else []
        def proj_unit(g, c, slot, wnm, wcnt):
            pk = B_PRJ[c % 2]

            def tb(t):
                w(t, wnm, wcnt)
                wait_free(t, ("xnrdy_a", X, c))
                wait_free(t, ("xnrdy_v", X, c))
                wait_free(t, ("bank", pk))
                wp = [whi[slot], wlo[slot]]
                mm3(t, pb[pk][:],
                    lambda fc, wi, wp=wp: wp[wi][:, fc * 512:(fc + 1) * 512],
                    c * 128, True, True, X)
                inc(LAST["t"], "t")
            blk.tensor(_wrap("t", tb))
            return C["t"], pk

        def evac_unit(g, c, pk, t_kvp):
            def vb(v):
                w(v, "t", t_kvp)
                wait_free(v, ("ksb", c % 2))
                wait_free(v, ("vsb", c % 2))
                v.tensor_tensor(k_sb[c % 2][:], pb[pk][:, 0:256],
                                bkv_bc[:, g * 256:(g + 1) * 256], ALU.add)
                pv = pb[pk][:].rearrange("p (h d) -> p h d", d=64)[:, 4:8, :]
                bv = bkv_bc[:].rearrange("p (h d) -> p h d", d=64)[
                    :, 16 + g * 4:16 + (g + 1) * 4, :]
                ov = v_sb[c % 2][:].rearrange("p (h d) -> p h d", d=65)[:, :, 0:64]
                v.tensor_tensor(ov, pv, bv, ALU.add)
                inc(LAST["v"], "v")
            blk.vector(_wrap("v", vb))
            set_free(("bank", pk), "v")
            return C["v"]

        def kcs_unit(g, c, cg, v_evac):
            def ab(a):
                w(a, "v", v_evac)
                wait_free(a, ("kcsa", c % 2))
                iv = k_sb[c % 2][:].rearrange("p (h d) -> p h d", d=64)
                oc = kcs_all[c % 2][:].rearrange("p (h d) -> p h d", d=128)[
                    :, :, 0:64]
                os_ = kcs_all[c % 2][:].rearrange("p (h d) -> p h d", d=128)[
                    :, :, 64:128]
                a.activation(out=oc, in_=iv, func=AF.Relu,
                             scale=coscols[:, cg:cg + 1])
                a.activation(out=os_, in_=iv, func=AF.Relu,
                             scale=sincols[:, cg:cg + 1])
                inc(LAST["a"], "a")
            blk.scalar(_wrap("a", ab))
            return C["a"]

        def kvmm_unit(g, c, cg, X, a_kcs):
            def tb2(t):
                w(t, "a", a_kcs)
                for hl in range(4):
                    h = g * 4 + hl
                    if h < 7:
                        bk, col, hlast = B_KV[0], h * 65, 6
                    elif h < 14:
                        bk, col, hlast = B_KV[1], (h - 7) * 65, 13
                    else:
                        bk, col, hlast = B_KV[2], (h - 14) * 65, 15
                    first = not kv_started.get((s, bk), False)
                    kv_started[(s, bk)] = True
                    if first:
                        wait_free(t, ("bank", bk))
                    last = (X == 1 and c == NCH - 1 and h == hlast)
                    t.matmul(pb[bk][:, col:col + 65],
                             kcs_all[c % 2][:, hl * 128:(hl + 1) * 128],
                             v_sb[c % 2][:, hl * 65:hl * 65 + 65],
                             start=first, stop=last)
                inc(LAST["t"], "t")
            blk.tensor(_wrap("t", tb2))
            set_free(("kcsa", c % 2), "t")
            set_free(("ksb", c % 2), "t")
            set_free(("vsb", c % 2), "t")

        def gate_unit(g, c, cg):
            def tg(t):
                wait_free(t, ("bank", B_G))
                gp = [ghi, glo]
                n = 0
                for (xs_, wi) in [(xnh[X], 0), (xnh[X], 1), (xnl[X], 0)]:
                    for fc in range(8):
                        t.matmul(pb[B_G][:, 0:8],
                                 xs_[:, fc * HALF + c * 128:fc * HALF + c * 128 + 128],
                                 gp[wi][:, fc * 8:(fc + 1) * 8],
                                 start=(n == 0), stop=(n == 23))
                        n += 1
                inc(LAST["t"], "t")
            blk.tensor(_wrap("t", tg))
            t_g = C["t"]

            def vg(v):
                w(v, "t", t_g)
                v.reduce_max(sstdt[:, 8 * c:8 * c + 1], pb[B_G][:, 0:8], axis=AX.X)
                v.drain()
                v.tensor_scalar(masks[:, cg * 8:cg * 8 + 8], pb[B_G][:, 0:8],
                                sstdt[:, 8 * c:8 * c + 1], None, ALU.is_equal)
                inc(LAST["v"], "v")
            blk.vector(_wrap("v", vg))
            set_free(("bank", B_G), "v")

        for g in range(4):
            slot = g % 2
            wnm, wcnt = stream_pair(blk, slot, g)

            pend = None
            for c in range(NCH):
                cg = X * 8 + c
                for _ in range(2):
                    if side:
                        side.pop(0)()
                t_kvp, pk = proj_unit(g, c, slot, wnm, wcnt)
                if c >= 6:
                    set_free(("whi", slot), "t")
                    set_free(("wlo", slot), "t")
                if pend is not None:
                    pc, pcg, pa = pend
                    kvmm_unit(g, pc, pcg, X, pa)
                    if g == 3:
                        gate_unit(g, pc, pcg)
                v_evac = evac_unit(g, c, pk, t_kvp)
                a_kcs = kcs_unit(g, c, cg, v_evac)
                pend = (c, cg, a_kcs)
            pc, pcg, pa = pend
            kvmm_unit(g, pc, pcg, X, pa)
            if g == 3:
                gate_unit(g, pc, pcg)
        while side:
            side.pop(0)()

    def emit_kv_end():
        set_free(("xnfm", 0), "t")

    def emit_kvn_evac(blk, s):
        def vb(v):
            w(v, "t", C["t"])
            wait_free(v, ("kvnused", 0))
            v.tensor_copy(kvn[:, 0:455], pb[B_KV[0]][:, 0:455])
            v.tensor_copy(kvn[:, 455:910], pb[B_KV[1]][:, 0:455])
            v.tensor_copy(kvn[:, 910:1040], pb[B_KV[2]][:, 0:130])
            inc(LAST["v"], "v")
        blk.vector(_wrap("v", vb))
        v_kvn = C["v"]
        for bk in B_KV:
            kv_started[(s, bk)] = False
            set_free(("bank", bk), "v")

        def gb(g):
            w(g, "v", v_kvn)
            wait_free(g, ("kvnused", 0))
            inc(g.dma_start(out=kvn_sw[0:64, :], in_=kvn[64:128, :]), "skv", 16)
            inc(g.dma_start(out=kvn_sw[64:128, :], in_=kvn[0:64, :]), "skv", 16)
        blk.gpsimd(_wrap("g", gb))
        set_free(("kvnrdy", 0), "skv")

    B_Q = (6, 7)

    def emit_p2compute(blk, s, X, side=None, side_pre=None):
        for u in (side_pre or []):
            u()
        side = list(side) if side else []
        """Per window: a batch of expert slabs, then the attention window with
        Wq/Wout streamed through the same 3-slot slab ring (the PE is busy
        with q/Wout matmuls while those slots are occupied)."""
        ring = {"pos": 0}

        def next_slot():
            slotv = ring["pos"] % 3
            ring["pos"] += 1
            return slotv

        def exp_slab_units(j):
            e, fs = j // 2, j % 2
            slot = next_slot()
            wnm, wcnt = stream_pair(blk, slot, 8 + j)
            for c in range(NCH):
                cg = X * 8 + c
                pk = B_PRJ[c % 2]

                def tb(t, c=c, pk=pk, slot=slot, wnm=wnm, wcnt=wcnt):
                    w(t, wnm, wcnt)
                    wait_free(t, ("bank", pk))
                    wp = [whi[slot], wlo[slot]]
                    mm3(t, pb[pk][:],
                        lambda fc, wi, wp=wp: wp[wi][:, fc * 512:(fc + 1) * 512],
                        c * 128, True, True, X)
                    inc(LAST["t"], "t")
                blk.tensor(_wrap("t", tb))
                t_y = C["t"]
                if c >= 6:
                    set_free(("whi", slot), "t")
                    set_free(("wlo", slot), "t")

                def vb(v, e=e, fs=fs, c=c, cg=cg, pk=pk, t_y=t_y):
                    w(v, "t", t_y)
                    sl = acc[:, c * E + fs * 512:c * E + (fs + 1) * 512]
                    v.scalar_tensor_tensor(sl, pb[pk][:],
                                           masks[:, cg * 8 + e:cg * 8 + e + 1],
                                           sl, op0=ALU.mult, op1=ALU.add)
                    if c == NCH - 1:
                        v.drain()
                    inc(LAST["v"], "v")
                blk.vector(_wrap("v", vb))
                set_free(("bank", pk), "v")

        for wdx in range(4):
            gw = X * 4 + wdx
            nside = len(side) if wdx == 3 else (len(side) + 3 - wdx) // (4 - wdx)
            for _ in range(nside):
                side.pop(0)()
            for j in range(4 * wdx, 4 * wdx + 4):
                exp_slab_units(j)

            sq0, sq1 = next_slot(), next_slot()
            qn0, qv0 = stream_pair(blk, sq0, 4)
            qn1, qv1 = stream_pair(blk, sq1, 5)
            whiQ = [whi[sq0], whi[sq1]]
            wloQ = [wlo[sq0], wlo[sq1]]

            def strig(e, gw=gw):
                wait_free(e, ("trig", 0))
                off = gw * 256
                inc(e.dma_start(out=coswin[:],
                                in_=trig_h[0, off:off + 256].rearrange(
                                    "(a b) -> a b", a=1).partition_broadcast(128)),
                    "str", 16)
                inc(e.dma_start(out=sinwin[:],
                                in_=trig_h[1, off:off + 256].rearrange(
                                    "(a b) -> a b", a=1).partition_broadcast(128)),
                    "str", 16)
            blk.sync(strig)
            s_trig = C["str"]

            v_atp_d = {}

            def tail_ctx(qc, v_qcs):
                for hh in range(2):
                    h = 2 * qc + hh
                    cb = B_KV[hh]

                    def tc(t, qc=qc, hh=hh, h=h, cb=cb, v_qcs=v_qcs):
                        w(t, "v", v_qcs)
                        wait_free(t, ("bank", cb))
                        wait_free(t, ("kvnrdy", 0))
                        kc = kvn if hh == 0 else kvn_sw
                        ks = kvn_sw if hh == 0 else kvn
                        b0 = hh * 64
                        for ts in range(2):
                            t.matmul(pb[cb][0:128, ts * 65:ts * 65 + 65],
                                     qcos[qc % 2][b0:b0 + 64,
                                                  ts * 128:(ts + 1) * 128],
                                     kc[b0:b0 + 64, h * 65:h * 65 + 65],
                                     start=True, stop=False)
                            t.matmul(pb[cb][0:128, ts * 65:ts * 65 + 65],
                                     qsin[qc % 2][b0:b0 + 64,
                                                  ts * 128:(ts + 1) * 128],
                                     ks[b0:b0 + 64, h * 65:h * 65 + 65],
                                     start=False, stop=True)
                        inc(LAST["t"], "t")
                    blk.tensor(_wrap("t", tc))
                    t_ctx = C["t"]
                    if hh == 1:
                        set_free(("qcs", qc % 2), "t")

                    def vc(v, hh=hh, cb=cb, t_ctx=t_ctx):
                        w(v, "t", t_ctx)
                        pv = pb[cb][:, 0:130].rearrange("p (ts u) -> p ts u", u=65)[
                            :, :, 64:65]
                        nv = nrm4[:, hh * 4:hh * 4 + 4].rearrange(
                            "p (a u) -> p a u", u=2)[:, :, 0:1]
                        v.tensor_scalar_add(nv, pv, AT_EPS)
                        v.drain()
                        v.reciprocal(rcp4[:, hh * 4:hh * 4 + 4].rearrange(
                            "p (a u) -> p a u", u=2)[:, :, 0:1], nv)
                        v.drain()
                        inc(LAST["v"], "v")
                    blk.vector(_wrap("v", vc))

                    for ts in range(2):
                        def va(v, qc=qc, cb=cb, hh=hh, ts=ts):
                            if hh == 0:
                                wait_free(v, ("atile", qc % 2, ts))
                            v.tensor_scalar_mul(
                                atile[qc % 2][ts][:, hh * 64:hh * 64 + 64],
                                pb[cb][:, ts * 65:ts * 65 + 64],
                                rcp4[:, hh * 4 + ts * 2:hh * 4 + ts * 2 + 1])
                            inc(LAST["v"], "v")
                        blk.vector(_wrap("v", va))
                    free[("bank", cb)] = ("v", C["v"])
                v_atp_d[qc] = C["v"]

            def tail_atp(qc):
                v_atp = v_atp_d[qc]
                patp = B_TR[qc % 2]
                for ts in range(2):
                    def ta(t, qc=qc, ts=ts, patp=patp, v_atp=v_atp):
                        w(t, "v", v_atp)
                        if ts == 0:
                            wait_free(t, ("bank", patp))
                        t.transpose(pb[patp][0:128, ts * 128:(ts + 1) * 128],
                                    atile[qc % 2][ts][:], ident[:])
                        inc(LAST["t"], "t")
                    blk.tensor(_wrap("t", ta))
                    set_free(("atile", qc % 2, ts), "t")
                t_atp = C["t"]

                def aatp(a, qc=qc, patp=patp, t_atp=t_atp):
                    w(a, "t", t_atp)
                    wait_free(a, ("afm", qc))
                    a.activation(out=afm_hi[:, qc * 256:(qc + 1) * 256],
                                 in_=pb[patp][:, 0:256], func=AF.Copy)
                    inc(LAST["a"], "a")
                blk.scalar(_wrap("a", aatp))
                a_afm = C["a"]

                def vatp(v, qc=qc, patp=patp, t_atp=t_atp, a_afm=a_afm):
                    w(v, "t", t_atp)
                    w(v, "a", a_afm)
                    v.tensor_sub(afm_lo[:, qc * 256:(qc + 1) * 256],
                                 pb[patp][:, 0:256],
                                 afm_hi[:, qc * 256:(qc + 1) * 256])
                    inc(LAST["v"], "v")
                blk.vector(_wrap("v", vatp))
                set_free(("bank", patp), "v")
                free[("attnfm_rdy", qc)] = ("v", C["v"])

            pend_ctx = None
            pend_atp = None
            for qc in range(8):
                pq = B_Q[qc % 2]

                def tq(t, qc=qc, wdx=wdx, pq=pq):
                    w(t, qn0, qv0)
                    w(t, qn1, qv1)
                    wait_free(t, ("bank", pq))
                    n = 0
                    for (xi, wi) in [(0, 0), (0, 1), (1, 0)]:
                        for fc in range(8):
                            wp = [whiQ, wloQ][wi][fc // 4]
                            lhsT = wp[:, (fc % 4) * 1024 + qc * 128:
                                      (fc % 4) * 1024 + (qc + 1) * 128]
                            xs_ = [xnh[X], xnl[X]][xi]
                            rr = xs_[:, fc * HALF + wdx * 256:fc * HALF + (wdx + 1) * 256]
                            t.matmul(pb[pq][:, 0:256], lhsT, rr,
                                     start=(n == 0), stop=(n == 23))
                            n += 1
                    inc(LAST["t"], "t")
                blk.tensor(_wrap("t", tq))
                t_qp = C["t"]
                if qc == 7:
                    set_free(("whi", sq0), "t")
                    set_free(("wlo", sq0), "t")
                    set_free(("whi", sq1), "t")
                    set_free(("wlo", sq1), "t")

                def aq2(a, qc=qc, pq=pq, t_qp=t_qp):
                    w(a, "t", t_qp)
                    wait_free(a, ("qrelu", qc % 2))
                    a.activation(out=qrelu[qc % 2][:], in_=pb[pq][:, 0:256],
                                 func=AF.Relu, bias=bq_cols[:, qc:qc + 1])
                    inc(LAST["a"], "a")
                blk.scalar(_wrap("a", aq2))
                a_qr = C["a"]
                set_free(("bank", pq), "a")

                def vq2(v, qc=qc, a_qr=a_qr, s_trig=s_trig):
                    w(v, "a", a_qr)
                    w(v, "str", s_trig)
                    wait_free(v, ("qcs", qc % 2))
                    v.tensor_mul(qcos[qc % 2][:], qrelu[qc % 2][:], coswin[:])
                    v.tensor_mul(qsin[qc % 2][:], qrelu[qc % 2][:], sinwin[:])
                    inc(LAST["v"], "v")
                blk.vector(_wrap("v", vq2))
                v_qcs = C["v"]
                set_free(("qrelu", qc % 2), "v")
                if qc == 7:
                    set_free(("trig", 0), "v")

                if pend_ctx is not None:
                    tail_ctx(*pend_ctx)
                if pend_atp is not None:
                    tail_atp(pend_atp)
                pend_atp = pend_ctx[0] if pend_ctx is not None else None
                pend_ctx = (qc, v_qcs)
            tail_ctx(*pend_ctx)
            if pend_atp is not None:
                tail_atp(pend_atp)
            tail_atp(pend_ctx[0])

            so0_, so1_ = next_slot(), next_slot()
            on0, ov0 = stream_pair(blk, so0_, 6)
            on1, ov1 = stream_pair(blk, so1_, 7)
            whiO = [whi[so0_], whi[so1_]]
            wloO = [wlo[so0_], wlo[so1_]]
            for ts in range(2):
                for of in range(2):
                    pw = B_TR[(ts * 2 + of) % 2]

                    def tw(t, wdx=wdx, ts=ts, of=of, pw=pw):
                        w(t, on0, ov0)
                        w(t, on1, ov1)
                        for qc in range(8):
                            nm, val = free[("attnfm_rdy", qc)]
                            w(t, nm, val)
                        wait_free(t, ("bank", pw))
                        n = 0
                        for (xi, wi) in [(0, 0), (0, 1), (1, 0)]:
                            for ec in range(8):
                                af = [afm_hi, afm_lo][xi]
                                wp = [whiO, wloO][wi][ec // 4]
                                t.matmul(pb[pw][:],
                                         af[:, ec * 256 + ts * 128:ec * 256 + (ts + 1) * 128],
                                         wp[:, (ec % 4) * 1024 + of * 512:
                                            (ec % 4) * 1024 + (of + 1) * 512],
                                         start=(n == 0), stop=(n == 23))
                                n += 1
                        inc(LAST["t"], "t")
                    blk.tensor(_wrap("t", tw))
                    t_wo = C["t"]
                    if ts == 1 and of == 1:
                        set_free(("whi", so0_), "t")
                        set_free(("wlo", so0_), "t")
                        set_free(("whi", so1_), "t")
                        set_free(("wlo", so1_), "t")

                    def vw(v, wdx=wdx, ts=ts, of=of, pw=pw, t_wo=t_wo):
                        w(v, "t", t_wo)
                        sl = acc[:, (wdx * 2 + ts) * E + of * 512:
                                 (wdx * 2 + ts) * E + (of + 1) * 512]
                        v.tensor_add(sl, sl, pb[pw][:])
                        if ts == 1 and of == 1:
                            v.drain()
                        inc(LAST["v"], "v")
                    blk.vector(_wrap("v", vw))
                    set_free(("bank", pw), "v")
            for qc in range(8):
                free[("afm", qc)] = ("t", C["t"])
        set_free(("xnfm", X), "t")
        set_free(("kvnused", 0), "t")

    def out_units(blk, s, X):
        units = []

        def mk(c):
            def run():
                def vb(v, c=c):
                    wait_free(v, ("xbuf", c % 2))
                    if c == 0:
                        v.drain()
                    v.tensor_add(out_sb[c % 2][:], acc[:, c * E:(c + 1) * E],
                                 bout_bc[:])
                    inc(LAST["v"], "v")
                blk.vector(_wrap("v", vb))
                v_o = C["v"]

                def syb(e, c=c, v_o=v_o):
                    w(e, "v", v_o)
                    inc(e.dma_start(
                        out=out_h[s, X * HALF + c * 128:X * HALF + (c + 1) * 128, :],
                        in_=out_sb[c % 2][:]),
                        "so%d" % (c % 2), 16)
                blk.sync(syb)
                set_free(("xbuf", c % 2), "so%d" % (c % 2))
            return run

        for c in range(NCH):
            units.append(mk(c))
        return units

    def emit_out(blk, s, X):
        for u in out_units(blk, s, X):
            u()

    uA_next = None
    for s in range(2):
        uA = prep_units(blk, s, 0, hooks_in_b=False) if uA_next is None else uA_next
        if s == 0:
            for u in uA[:NCH]:
                u()
        uB = prep_units(blk, s, 1)
        emit_kv(blk, s, 0, side=(uA[NCH:] + uB) if s == 0 else uB)
        while presplit_pending:
            presplit_unit(presplit_pending.pop(0), 2)
        presplit_flush()
        if ("accrdy", 0) not in free:
            free[("accrdy", 0)] = ("sq0", C["sq0"])
            free[("accrdy", 1)] = ("sq1", C["sq1"])
        ai0 = accinit_units(blk, s, 0)
        emit_kv(blk, s, 1, side=ai0)
        emit_kvn_evac(blk, s)
        side1 = accinit_units(blk, s, 1)
        if s == 0:
            uA_next = prep_units(blk, s + 1, 0)
        emit_p2compute(blk, s, 0, side=(uA_next[:9] if s == 0 else None))
        emit_p2compute(blk, s, 1, side=(uA_next[9:] if s == 0 else None),
                       side_pre=out_units(blk, s, 0) + side1)
        if s == 0:
            uA_next = []
        emit_out(blk, s, 1)

    def syfin(e):
        w(e, "so0", C["so0"])
        w(e, "so1", C["so1"])
    blk.sync(syfin)

    blk_cm.__exit__(None, None, None)
    st.close()
    return nc


_NC_CACHE = None


def kernel(L0_attn_x, L0_attn_W_qkv, L0_attn_b_qkv, L0_attn_W_out, L0_attn_b_out,
           L0_ffn_gate_w, L0_ffn_expert_w):
    global _NC_CACHE
    x = np.ascontiguousarray(np.asarray(L0_attn_x, dtype=np.float32))
    wqkv = np.asarray(L0_attn_W_qkv, dtype=np.float32)
    gate = np.asarray(L0_ffn_gate_w, dtype=np.float32)
    wqkvg = np.ascontiguousarray(np.concatenate([wqkv, gate.T], axis=1))
    wout = np.ascontiguousarray(np.asarray(L0_attn_W_out, dtype=np.float32))
    bqkv = np.ascontiguousarray(np.asarray(L0_attn_b_qkv, dtype=np.float32))
    bout = np.ascontiguousarray(np.asarray(L0_attn_b_out, dtype=np.float32))
    wexp = np.ascontiguousarray(
        np.asarray(L0_ffn_expert_w, dtype=np.float32).transpose(0, 2, 1))
    with jax.default_device(jax.local_devices(backend="cpu")[0]):
        positions = jnp.arange(L, dtype=jnp.float32)
        trig = np.ascontiguousarray(np.stack([
            np.asarray(jnp.cos(jnp.pi / 2 * positions / L), dtype=np.float32),
            np.asarray(jnp.sin(jnp.pi / 2 * positions / L), dtype=np.float32)]))

    if _NC_CACHE is None:
        _NC_CACHE = build()
    nc = _NC_CACHE

    in_maps = []
    for i in range(8):
        in_maps.append({
            "x": x[2 * i:2 * i + 2],
            "wqkvg": wqkvg,
            "wout": wout,
            "bqkv": bqkv,
            "bout": bout,
            "wexp": wexp,
            "trig": trig,
        })
    res = run_bass_kernel_spmd(nc, in_maps, core_ids=list(range(8)))
    out = np.concatenate([r["out"] for r in res.results], axis=0)
    return out.reshape(16, 2048, 1024)


# version 20 (final)
# speedup vs baseline: 1.1012x; 1.0404x over previous
import contextlib
import numpy as np
import jax
import jax.numpy as jnp
import concourse.bass as bass
import concourse.mybir as mybir
from concourse.bass_utils import run_bass_kernel_spmd

F32 = mybir.dt.float32
BF16 = mybir.dt.bfloat16
AF = mybir.ActivationFunctionType
ALU = mybir.AluOpType
AX = mybir.AxisListType

PI = float(np.pi)
C_HI = float(np.float32(np.pi / 4096.0))
C_LO = float(np.float32(np.pi / 4096.0 - C_HI))
PH_HI = float(np.float32(np.pi / 2.0))
PH_LO = float(np.float32(np.pi / 2.0 - PH_HI))
E = 1024
NH = 16
HD = 64
L = 2048
HALF = 1024
NCH = 8
LN_EPS = 1e-5
AT_EPS = 1e-4


def build():
    nc = bass.Bass("TRN2", target_bir_lowering=False)

    x_h = nc.dram_tensor("x", [2, L, E], F32, kind="ExternalInput").ap()
    wqkvg_h = nc.dram_tensor("wqkvg", [E, 3080], F32, kind="ExternalInput").ap()
    wout_h = nc.dram_tensor("wout", [E, E], F32, kind="ExternalInput").ap()
    bqkv_h = nc.dram_tensor("bqkv", [3072], F32, kind="ExternalInput").ap()
    bout_h = nc.dram_tensor("bout", [E], F32, kind="ExternalInput").ap()
    wexp_h = nc.dram_tensor("wexp", [8, E, E], F32, kind="ExternalInput").ap()
    trig_h = nc.dram_tensor("trig", [2, L], F32, kind="ExternalInput").ap()
    out_h = nc.dram_tensor("out", [2, L, E], F32, kind="ExternalOutput").ap()
    scr_hi = nc.dram_tensor("scr_hi", [24, 128, 4096], BF16, kind="Internal").ap()
    scr_lo = nc.dram_tensor("scr_lo", [24, 128, 4096], BF16, kind="Internal").ap()

    st = contextlib.ExitStack()

    def sb(name, shape, dt=F32):
        return st.enter_context(nc.sbuf_tensor(name, shape, dt))

    xbuf = [sb(f"xbuf{i}", [128, E]) for i in range(2)]
    xnbuf = [sb("xnbuf0", [128, E])]
    xnbuf.append(xnbuf[0])
    out_sb = xbuf
    acc = sb("acc", [128, NCH * E])
    xnh = [sb(f"xnh{i}", [128, 8 * HALF], BF16) for i in range(2)]
    xnl = [sb(f"xnl{i}", [128, 8 * HALF], BF16) for i in range(2)]
    afm_hi = sb("afm_hi", [128, 8 * 256], BF16)
    afm_lo = sb("afm_lo", [128, 8 * 256], BF16)

    whi = [sb(f"whi{i}", [128, 4096], BF16) for i in range(3)]
    wlo = [sb(f"wlo{i}", [128, 4096], BF16) for i in range(3)]

    ghi = sb("ghi", [128, 64], BF16)
    glo = sb("glo", [128, 64], BF16)

    k_sb = [sb(f"ksb{i}", [128, 256]) for i in range(2)]
    v_sb = [sb(f"vsb{i}", [128, 4 * 65]) for i in range(2)]
    kcs_all = [sb(f"kcsa{i}", [128, 512]) for i in range(2)]
    gsb = kcs_all[0]
    kvn = sb("kvn", [128, NH * 65])
    kvn_sw = sb("kvn_sw", [128, NH * 65])

    qrelu = [sb(f"qrelu{i}", [128, 256]) for i in range(2)]
    qcos = [sb(f"qcos{i}", [128, 256]) for i in range(2)]
    qsin = [sb(f"qsin{i}", [128, 256]) for i in range(2)]
    atile = [[sb(f"atile{i}_{j}", [128, 128]) for j in range(2)] for i in range(2)]
    nrm4 = sb("nrm4", [128, 8])
    rcp4 = sb("rcp4", [128, 8])

    sumt = sb("sumt", [128, 64])
    sqt = sb("sqt", [128, 64])
    sq2t = sb("sq2t", [128, 64])
    vart = sb("vart", [128, 64])
    m2t = sumt
    sstdt = sb("sstdt", [128, 64])
    mean_t = sb("mean_t", [128, 128])
    rstd_t = sb("rstd_t", [128, 128])
    masks = sb("masks", [128, 128])

    ident = sb("ident", [128, 128])
    idp = atile[0][0]
    idf = atile[0][1]
    sincols = sb("sincols", [128, 16])
    coscols = sb("coscols", [128, 16])
    coswin = sb("coswin", [128, 256])
    sinwin = sb("sinwin", [128, 256])
    ones1 = sb("ones1", [1, 128])
    eps5 = sb("eps5", [128, 1])
    bq_cols = sb("bq_cols", [128, 8])
    bkv_bc = sb("bkv_bc", [128, 2048])
    bout_bc = sb("bout_bc", [128, 1024])

    pb = [st.enter_context(nc.psum_tensor(f"pb{i}", [128, 512], F32))
          for i in range(8)]
    B_TR = (0, 1)
    B_PRJ = (2, 3)
    B_KV = (4, 5, 6)
    B_G = 7

    S = {}
    C = {}
    for name in ["sx0", "sx1", "sw0", "sw1", "sw2", "so0", "so1", "sb", "skv",
                 "swq", "swo", "sq0", "sq1", "str", "t", "v", "a", "g"]:
        S[name] = st.enter_context(nc.semaphore(name))
        C[name] = 0

    def inc(instr, name, n=1):
        instr.then_inc(S[name], n)
        C[name] += n
        return C[name]

    def w(eng, name, val):
        if val > 0:
            eng.wait_ge(S[name], val)

    free = {}

    def wait_free(eng, key):
        if key in free:
            nm, val = free[key]
            w(eng, nm, val)

    def set_free(key, nm):
        free[key] = (nm, C[nm])

    kv_started = {}

    LAST = {}

    class _EP:
        def __init__(self, name, eng):
            self._n, self._e = name, eng

        def __getattr__(self, k):
            val = getattr(self._e, k)
            if not callable(val):
                return val

            def wrapped(*a, **kw):
                r = val(*a, **kw)
                if hasattr(r, "then_inc"):
                    LAST[self._n] = r
                return r
            return wrapped

    def _wrap(name, f):
        return lambda e, f=f, name=name: f(_EP(name, e))

    blk_cm = nc.Block()
    blk = blk_cm.__enter__()
    if True:
        def gbody(g):
            inc(g.iota(idp[:], [[0, 128]], channel_multiplier=1,
                       allow_small_or_imprecise_dtypes=True), "g")
            inc(g.iota(idf[:], [[1, 128]], channel_multiplier=0,
                       allow_small_or_imprecise_dtypes=True), "g")
            src = wqkvg_h.rearrange("(fc p) c -> p fc c", p=128)[:, :, 3072:3080]
            inc(g.dma_start(out=gsb[:, 0:64].rearrange("p (fc e) -> p fc e", e=8),
                            in_=src), "sb", 16)
            inc(g.dma_start(out=acc[0:1, 0:2048],
                            in_=bqkv_h[1024:3072].rearrange("(a b) -> a b", a=1)),
                "sb", 16)
            inc(g.dma_start(out=acc[0:1, 2048:3072],
                            in_=bout_h[:].rearrange("(a b) -> a b", a=1)),
                "sb", 16)
            inc(g.dma_start(out=bq_cols[:],
                            in_=bqkv_h[0:1024].rearrange("(c p) -> p c", p=128),
                            allow_slow_non_contiguous=True),
                "sb", 16)
            inc(g.dma_start(out=coscols[:],
                            in_=trig_h[0].rearrange("(c p) -> p c", p=128),
                            allow_slow_non_contiguous=True),
                "sb", 16)
            inc(g.dma_start(out=sincols[:],
                            in_=trig_h[1].rearrange("(c p) -> p c", p=128),
                            allow_slow_non_contiguous=True),
                "sb", 16)
        blk.gpsimd(_wrap("g", gbody))
        sw_setup = C["sb"]

        def vbody(v):
            w(v, "g", 2)
            v.tensor_tensor(ident[:], idp[:], idf[:], ALU.is_equal)
            v.memset(ones1[:], 1.0)
            v.memset(eps5[:], LN_EPS)
            v.memset(v_sb[0][:].rearrange("p (h c) -> p h c", c=65)[:, :, 64:65], 1.0)
            v.drain()
            v.memset(v_sb[1][:].rearrange("p (h c) -> p h c", c=65)[:, :, 64:65], 1.0)
            inc(LAST["v"], "v")
        blk.vector(_wrap("v", vbody))
        v_consts = C["v"]

        def abody(a):
            w(a, "v", v_consts)
            w(a, "sb", sw_setup)
            a.activation(out=ghi[:], in_=gsb[:, 0:64], func=AF.Copy)
            inc(LAST["a"], "a")
        blk.scalar(_wrap("a", abody))
        a_setup = C["a"]

        def v2body(v):
            w(v, "a", a_setup)
            v.tensor_sub(glo[:], gsb[:, 0:64], ghi[:])
            inc(LAST["v"], "v")
        blk.vector(_wrap("v", v2body))

        def tbody(t):
            w(t, "sb", sw_setup)
            w(t, "v", v_consts)
            for i in range(2):
                t.matmul(pb[i][:], ones1[:, 0:128], acc[0:1, i * 512:(i + 1) * 512],
                         start=True, stop=True)
            inc(LAST["t"], "t")
        blk.tensor(_wrap("t", tbody))
        t_b1 = C["t"]

        def v3body(v):
            w(v, "t", t_b1)
            v.tensor_copy(bkv_bc[:, 0:512], pb[0][:])
            v.tensor_copy(bkv_bc[:, 512:1024], pb[1][:])
            inc(LAST["v"], "v")
        blk.vector(_wrap("v", v3body))
        v_b1 = C["v"]

        def t2body(t):
            w(t, "v", v_b1)
            for i in range(2):
                t.matmul(pb[i][:], ones1[:, 0:128],
                         acc[0:1, 1024 + i * 512:1024 + (i + 1) * 512],
                         start=True, stop=True)
            inc(LAST["t"], "t")
        blk.tensor(_wrap("t", t2body))
        t_b2 = C["t"]

        def v4body(v):
            w(v, "t", t_b2)
            v.tensor_copy(bkv_bc[:, 1024:1536], pb[0][:])
            v.tensor_copy(bkv_bc[:, 1536:2048], pb[1][:])
            inc(LAST["v"], "v")
        blk.vector(_wrap("v", v4body))
        v_b2 = C["v"]

        def t3body(t):
            w(t, "v", v_b2)
            for i in range(2):
                t.matmul(pb[i][:], ones1[:, 0:128],
                         acc[0:1, 2048 + i * 512:2048 + (i + 1) * 512],
                         start=True, stop=True)
            inc(LAST["t"], "t")
        blk.tensor(_wrap("t", t3body))
        t_b3 = C["t"]

        def v5body(v):
            w(v, "t", t_b3)
            v.tensor_copy(bout_bc[:, 0:512], pb[0][:])
            v.tensor_copy(bout_bc[:, 512:1024], pb[1][:])
            inc(LAST["v"], "v")
        blk.vector(_wrap("v", v5body))
        set_free(("accrdy", 0), "t")
        v_bias = C["v"]

        kin0 = wqkvg_h.rearrange("(fc p) c -> p fc c", p=128)
        wo0 = wout_h.rearrange("(ec p) f -> p ec f", p=128)
        slab_src = []
        for g in range(4):
            slab_src.append([
                (512, 0, 256, kin0[:, :, 1024 + g * 256:1024 + (g + 1) * 256]),
                (512, 256, 512, kin0[:, :, 2048 + g * 256:2048 + (g + 1) * 256]),
            ])
        for piece in range(2):
            slab_src.append([(1024, 0, 1024,
                              kin0[:, piece * 4:(piece + 1) * 4, 0:1024])])
        for piece in range(2):
            slab_src.append([(1024, 0, 1024,
                              wo0[:, piece * 4:(piece + 1) * 4, :])])
        for e in range(8):
            we = wexp_h[e].rearrange("(hc p) f -> p hc f", p=128)
            for fs in range(2):
                slab_src.append([(512, 0, 512,
                                  we[:, :, fs * 512:(fs + 1) * 512])])
        scr_ready = {}
        NSLAB = len(slab_src)
        assert NSLAB == 24

        def presplit_in(i, slot):
            stv = acc[:, (i % 2) * 4096:(i % 2) * 4096 + 4096]

            def gin(g, i=i, slot=slot, stv=stv):
                wait_free(g, ("pstg", i % 2))
                if i < 2 or i == 8:
                    w(g, "t", t_b3)
                w(g, "sw%d" % slot, C["sw%d" % slot])
                for (dsub, c0, c1, srcv) in slab_src[i]:
                    ov = stv.rearrange("p (fc d) -> p fc d", d=dsub)[:, :, c0:c1]
                    inc(g.dma_start(out=ov, in_=srcv), "sw%d" % slot, 16)
            blk.gpsimd(_wrap("g", gin))
            return C["sw%d" % slot]

        def presplit_out(i, slot, sw_in):
            stv = acc[:, (i % 2) * 4096:(i % 2) * 4096 + 4096]

            def asp(a, slot=slot, stv=stv, sw_in=sw_in):
                w(a, "sw%d" % slot, sw_in)
                wait_free(a, ("whi", slot))
                a.activation(out=whi[slot][:], in_=stv, func=AF.Copy)
                inc(LAST["a"], "a")
            blk.scalar(_wrap("a", asp))
            a_sp = C["a"]

            def vsp(v, slot=slot, stv=stv, sw_in=sw_in, a_sp=a_sp):
                w(v, "sw%d" % slot, sw_in)
                w(v, "a", a_sp)
                wait_free(v, ("wlo", slot))
                v.tensor_sub(wlo[slot][:], stv, whi[slot][:])
                inc(LAST["v"], "v")
            blk.vector(_wrap("v", vsp))
            v_sp = C["v"]

            def gout(g, i=i, slot=slot, a_sp=a_sp, v_sp=v_sp):
                w(g, "a", a_sp)
                w(g, "v", v_sp)
                w(g, "sq%d" % (i % 2), C["sq%d" % (i % 2)])
                inc(g.dma_start(out=scr_hi[i], in_=whi[slot][:]), "sq%d" % (i % 2), 16)
                inc(g.dma_start(out=scr_lo[i], in_=wlo[slot][:]), "sq%d" % (i % 2), 16)
            blk.gpsimd(_wrap("g", gout))
            scr_ready[i] = ("sq%d" % (i % 2), C["sq%d" % (i % 2)])
            set_free(("pstg", i % 2), "sq%d" % (i % 2))
            set_free(("whi", slot), "sq%d" % (i % 2))
            set_free(("wlo", slot), "sq%d" % (i % 2))

        _ps_inflight = []

        def presplit_unit(i, slot):
            sw_in = presplit_in(i, slot)
            _ps_inflight.append((i, slot, sw_in))
            if len(_ps_inflight) >= 2:
                presplit_out(*_ps_inflight.pop(0))

        def presplit_flush():
            while _ps_inflight:
                presplit_out(*_ps_inflight.pop(0))

        for i in range(2):
            presplit_unit(i, i % 2)
        presplit_pending = list(range(2, 24))
        free[("accrdy", 2)] = ("t", C["t"])

        accrdy_keys = {}


    def prep_units(blk, s, X, hooks_in_b=True, hooks_in_a=True):
        units = []

        def passA_chunk(c):
            cg = X * 8 + c

            def run():
                def syb(e, c=c):
                    wait_free(e, ("xbuf", c % 2))
                    inc(e.dma_start(out=xbuf[c % 2][:],
                                    in_=x_h[s, X * HALF + c * 128:X * HALF + (c + 1) * 128, :]),
                        "sx%d" % (c % 2), 16)
                blk.sync(syb)
                sxn = "sx%d" % (c % 2)
                sx_c = C[sxn]

                def ab(a, c=c, sxn=sxn, sx_c=sx_c):
                    w(a, sxn, sx_c)
                    wait_free(a, ("xnbuf", 0))
                    a.activation(out=xnbuf[c % 2][:], in_=xbuf[c % 2][:],
                                 func=AF.Square)
                    inc(LAST["a"], "a")
                blk.scalar(_wrap("a", ab))
                a_c = C["a"]

                def vb(v, c=c, sxn=sxn, sx_c=sx_c, a_c=a_c):
                    w(v, sxn, sx_c)
                    w(v, "a", a_c)
                    v.reduce_sum(sumt[:, 8 * c:8 * c + 1], xbuf[c % 2][:], axis=AX.X)
                    v.reduce_sum(sqt[:, 8 * c:8 * c + 1], xnbuf[c % 2][:], axis=AX.X)
                    inc(LAST["v"], "v")
                blk.vector(_wrap("v", vb))
                set_free(("xbuf", c % 2), "v")
                set_free(("xnbuf", 0), "v")
                if hooks_in_a and presplit_pending:
                    presplit_unit(presplit_pending.pop(0), 2)
                    if not presplit_pending:
                        presplit_flush()
                        free[("accrdy", 0)] = ("sq0", C["sq0"])
                        free[("accrdy", 1)] = ("sq1", C["sq1"])
            return run

        def stats_chain():
            def run():
                def vstats(v):
                    sv = lambda t_: t_[:].rearrange("p (c u) -> p c u", u=8)[:, :, 0:1]
                    mv = mean_t[:].rearrange("p (c u) -> p c u", u=8)[:, X * 8:X * 8 + 8, 0:1]
                    v.drain()
                    v.tensor_scalar_mul(mv, sv(sumt), 1.0 / E)
                    v.tensor_scalar_mul(sv(sq2t), sv(sqt), 1.0 / E)
                    v.drain()
                    v.tensor_mul(sv(m2t), mv, mv)
                    v.drain()
                    v.tensor_sub(sv(vart), sv(sq2t), sv(m2t))
                    v.drain()
                    inc(LAST["v"], "v")
                blk.vector(_wrap("v", vstats))
                v_var = C["v"]

                def astats(a):
                    w(a, "v", v_var)
                    a.activation(out=sstdt[:].rearrange("p (c u) -> p c u", u=8)[:, :, 0:1],
                                 in_=vart[:].rearrange("p (c u) -> p c u", u=8)[:, :, 0:1],
                                 func=AF.Sqrt, bias=eps5[:])
                    inc(LAST["a"], "a")
                blk.scalar(_wrap("a", astats))
                a_sstd = C["a"]

                def vrstd(v):
                    w(v, "a", a_sstd)
                    rv = rstd_t[:].rearrange("p (c u) -> p c u", u=8)[:, X * 8:X * 8 + 8, 0:1]
                    v.reciprocal(rv, sstdt[:].rearrange("p (c u) -> p c u", u=8)[:, :, 0:1])
                    v.drain()
                    inc(LAST["v"], "v")
                blk.vector(_wrap("v", vrstd))
            return run

        def passB_chunk(c):
            cg = X * 8 + c

            def run():
                def syb(e, c=c):
                    wait_free(e, ("xbuf", c % 2))
                    inc(e.dma_start(out=xbuf[c % 2][:],
                                    in_=x_h[s, X * HALF + c * 128:X * HALF + (c + 1) * 128, :]),
                        "sx%d" % (c % 2), 16)
                blk.sync(syb)
                sxn = "sx%d" % (c % 2)
                sx_c = C[sxn]

                def vb(v, c=c, cg=cg, sxn=sxn, sx_c=sx_c):
                    w(v, sxn, sx_c)
                    wait_free(v, ("xnbuf", 0))
                    v.tensor_scalar(xnbuf[c % 2][:], xbuf[c % 2][:],
                                    mean_t[:, 8 * cg:8 * cg + 1],
                                    rstd_t[:, 8 * cg:8 * cg + 1],
                                    ALU.subtract, ALU.mult)
                    inc(LAST["v"], "v")
                blk.vector(_wrap("v", vb))
                set_free(("xbuf", c % 2), "v")
                v_xn = C["v"]

                for b in range(2):
                    def tb(t, c=c, b=b, v_xn=v_xn):
                        w(t, "v", v_xn)
                        wait_free(t, ("bank", B_TR[b]))
                        for j in range(4):
                            fc = b * 4 + j
                            t.transpose(pb[B_TR[b]][:, j * 128:(j + 1) * 128],
                                        xnbuf[c % 2][:, fc * 128:(fc + 1) * 128],
                                        ident[:])
                        inc(LAST["t"], "t")
                    blk.tensor(_wrap("t", tb))
                    t_tr = C["t"]
                    if b == 1:
                        set_free(("xnbuf", 0), "t")

                    def ab(a, c=c, b=b, t_tr=t_tr):
                        w(a, "t", t_tr)
                        if c == 0 and b == 0:
                            wait_free(a, ("xnfm", X))
                        pv = pb[B_TR[b]][:].rearrange("p (j u) -> p j u", u=128)
                        ov = xnh[X][:].rearrange("p (fc u) -> p fc u", u=HALF)[
                            :, b * 4:(b + 1) * 4, c * 128:(c + 1) * 128]
                        a.activation(out=ov, in_=pv, func=AF.Copy)
                        inc(LAST["a"], "a")
                    blk.scalar(_wrap("a", ab))
                    a_hi = C["a"]

                    def vb2(v, c=c, b=b, t_tr=t_tr, a_hi=a_hi):
                        w(v, "t", t_tr)
                        w(v, "a", a_hi)
                        if c == 0 and b == 0:
                            wait_free(v, ("xnfm", X))
                        pv = pb[B_TR[b]][:].rearrange("p (j u) -> p j u", u=128)
                        hv = xnh[X][:].rearrange("p (fc u) -> p fc u", u=HALF)[
                            :, b * 4:(b + 1) * 4, c * 128:(c + 1) * 128]
                        lv = xnl[X][:].rearrange("p (fc u) -> p fc u", u=HALF)[
                            :, b * 4:(b + 1) * 4, c * 128:(c + 1) * 128]
                        v.tensor_tensor(lv, pv, hv, ALU.subtract)
                        inc(LAST["v"], "v")
                    blk.vector(_wrap("v", vb2))
                    set_free(("bank", B_TR[b]), "v")
                    if b == 1:
                        free[("xnrdy_a", X, c)] = ("a", C["a"])
                        free[("xnrdy_v", X, c)] = ("v", C["v"])
                if hooks_in_b and presplit_pending:
                    presplit_unit(presplit_pending.pop(0), 2)
                    if not presplit_pending:
                        presplit_flush()
                        free[("accrdy", 0)] = ("sq0", C["sq0"])
                        free[("accrdy", 1)] = ("sq1", C["sq1"])
            return run

        for c in range(NCH):
            units.append(passA_chunk(c))
        units.append(stats_chain())
        for c in range(NCH):
            units.append(passB_chunk(c))
        return units

    def accinit_units(blk, s, X):
        units = []

        def mk(c):
            cg = X * 8 + c

            def run():
                def syb(e, c=c):
                    wait_free(e, ("xbuf", c % 2))
                    inc(e.dma_start(out=xbuf[c % 2][:],
                                    in_=x_h[s, X * HALF + c * 128:X * HALF + (c + 1) * 128, :]),
                        "sx%d" % (c % 2), 16)
                blk.sync(syb)
                sxn = "sx%d" % (c % 2)
                sx_c = C[sxn]

                def vb(v, c=c, cg=cg, sxn=sxn, sx_c=sx_c):
                    w(v, sxn, sx_c)
                    wait_free(v, ("xnbuf", 0))
                    if c == 0:
                        v.drain()
                        wait_free(v, ("accrdy", 0))
                        wait_free(v, ("accrdy", 1))
                        wait_free(v, ("accrdy", 2))
                    v.tensor_scalar(xnbuf[c % 2][:], xbuf[c % 2][:],
                                    mean_t[:, 8 * cg:8 * cg + 1],
                                    rstd_t[:, 8 * cg:8 * cg + 1],
                                    ALU.subtract, ALU.mult)
                    v.drain()
                    v.tensor_add(acc[:, c * E:(c + 1) * E], xbuf[c % 2][:],
                                 xnbuf[c % 2][:])
                    inc(LAST["v"], "v")
                blk.vector(_wrap("v", vb))
                set_free(("xbuf", c % 2), "v")
                set_free(("xnbuf", 0), "v")
                if presplit_pending:
                    presplit_unit(presplit_pending.pop(0), 2)
                    if not presplit_pending:
                        presplit_flush()
                        free[("accrdy", 0)] = ("sq0", C["sq0"])
                        free[("accrdy", 1)] = ("sq1", C["sq1"])
            return run

        for c in range(NCH):
            units.append(mk(c))
        return units

    def stream_pair(blk, slot, i):
        swn = "sw%d" % slot

        def gb(g):
            wait_free(g, ("whi", slot))
            wait_free(g, ("wlo", slot))
            nm, val = scr_ready[i]
            w(g, nm, val)
            inc(g.dma_start(out=whi[slot][:], in_=scr_hi[i]), swn, 16)
            inc(g.dma_start(out=wlo[slot][:], in_=scr_lo[i]), swn, 16)
        blk.gpsimd(_wrap("g", gb))
        return swn, C[swn]

    def mm3(t, bank_ap, rhs_cols, xcol, first, last, sl):
        passes = [(xnh[sl], 0), (xnh[sl], 1), (xnl[sl], 0)]
        n = 0
        for (xs_, wi) in passes:
            for fc in range(8):
                t.matmul(bank_ap,
                         xs_[:, fc * HALF + xcol:fc * HALF + xcol + 128],
                         rhs_cols(fc, wi),
                         start=(first and n == 0), stop=(last and n == 23))
                n += 1

    def emit_kv(blk, s, X, side=None):
        side = list(side) if side else []
        def proj_unit(g, c, slot, wnm, wcnt):
            pk = B_PRJ[c % 2]

            def tb(t):
                w(t, wnm, wcnt)
                wait_free(t, ("xnrdy_a", X, c))
                wait_free(t, ("xnrdy_v", X, c))
                wait_free(t, ("bank", pk))
                wp = [whi[slot], wlo[slot]]
                mm3(t, pb[pk][:],
                    lambda fc, wi, wp=wp: wp[wi][:, fc * 512:(fc + 1) * 512],
                    c * 128, True, True, X)
                inc(LAST["t"], "t")
            blk.tensor(_wrap("t", tb))
            return C["t"], pk

        def evac_unit(g, c, pk, t_kvp):
            def vb(v):
                w(v, "t", t_kvp)
                wait_free(v, ("ksb", c % 2))
                wait_free(v, ("vsb", c % 2))
                v.tensor_tensor(k_sb[c % 2][:], pb[pk][:, 0:256],
                                bkv_bc[:, g * 256:(g + 1) * 256], ALU.add)
                pv = pb[pk][:].rearrange("p (h d) -> p h d", d=64)[:, 4:8, :]
                bv = bkv_bc[:].rearrange("p (h d) -> p h d", d=64)[
                    :, 16 + g * 4:16 + (g + 1) * 4, :]
                ov = v_sb[c % 2][:].rearrange("p (h d) -> p h d", d=65)[:, :, 0:64]
                v.tensor_tensor(ov, pv, bv, ALU.add)
                inc(LAST["v"], "v")
            blk.vector(_wrap("v", vb))
            set_free(("bank", pk), "v")
            return C["v"]

        def kcs_unit(g, c, cg, v_evac):
            def ab(a):
                w(a, "v", v_evac)
                wait_free(a, ("kcsa", c % 2))
                iv = k_sb[c % 2][:].rearrange("p (h d) -> p h d", d=64)
                oc = kcs_all[c % 2][:].rearrange("p (h d) -> p h d", d=128)[
                    :, :, 0:64]
                os_ = kcs_all[c % 2][:].rearrange("p (h d) -> p h d", d=128)[
                    :, :, 64:128]
                a.activation(out=oc, in_=iv, func=AF.Relu,
                             scale=coscols[:, cg:cg + 1])
                a.activation(out=os_, in_=iv, func=AF.Relu,
                             scale=sincols[:, cg:cg + 1])
                inc(LAST["a"], "a")
            blk.scalar(_wrap("a", ab))
            return C["a"]

        def kvmm_unit(g, c, cg, X, a_kcs):
            def tb2(t):
                w(t, "a", a_kcs)
                for hl in range(4):
                    h = g * 4 + hl
                    if h < 7:
                        bk, col, hlast = B_KV[0], h * 65, 6
                    elif h < 14:
                        bk, col, hlast = B_KV[1], (h - 7) * 65, 13
                    else:
                        bk, col, hlast = B_KV[2], (h - 14) * 65, 15
                    first = not kv_started.get((s, bk), False)
                    kv_started[(s, bk)] = True
                    if first:
                        wait_free(t, ("bank", bk))
                    last = (X == 1 and c == NCH - 1 and h == hlast)
                    t.matmul(pb[bk][:, col:col + 65],
                             kcs_all[c % 2][:, hl * 128:(hl + 1) * 128],
                             v_sb[c % 2][:, hl * 65:hl * 65 + 65],
                             start=first, stop=last)
                inc(LAST["t"], "t")
            blk.tensor(_wrap("t", tb2))
            set_free(("kcsa", c % 2), "t")
            set_free(("ksb", c % 2), "t")
            set_free(("vsb", c % 2), "t")

        def gate_unit(g, c, cg):
            def tg(t):
                wait_free(t, ("bank", B_G))
                gp = [ghi, glo]
                n = 0
                for (xs_, wi) in [(xnh[X], 0), (xnh[X], 1), (xnl[X], 0)]:
                    for fc in range(8):
                        t.matmul(pb[B_G][:, 0:8],
                                 xs_[:, fc * HALF + c * 128:fc * HALF + c * 128 + 128],
                                 gp[wi][:, fc * 8:(fc + 1) * 8],
                                 start=(n == 0), stop=(n == 23))
                        n += 1
                inc(LAST["t"], "t")
            blk.tensor(_wrap("t", tg))
            t_g = C["t"]

            def vg(v):
                w(v, "t", t_g)
                v.reduce_max(sstdt[:, 8 * c:8 * c + 1], pb[B_G][:, 0:8], axis=AX.X)
                v.drain()
                v.tensor_scalar(masks[:, cg * 8:cg * 8 + 8], pb[B_G][:, 0:8],
                                sstdt[:, 8 * c:8 * c + 1], None, ALU.is_equal)
                inc(LAST["v"], "v")
            blk.vector(_wrap("v", vg))
            set_free(("bank", B_G), "v")

        for g in range(4):
            slot = g % 2
            wnm, wcnt = stream_pair(blk, slot, g)

            pend = None
            for c in range(NCH):
                cg = X * 8 + c
                npop = 2 if len(side) > 17 else 1
                for _ in range(npop):
                    if side:
                        side.pop(0)()
                t_kvp, pk = proj_unit(g, c, slot, wnm, wcnt)
                if c >= 6:
                    set_free(("whi", slot), "t")
                    set_free(("wlo", slot), "t")
                if pend is not None:
                    pc, pcg, pa = pend
                    kvmm_unit(g, pc, pcg, X, pa)
                    if g == 3:
                        gate_unit(g, pc, pcg)
                v_evac = evac_unit(g, c, pk, t_kvp)
                a_kcs = kcs_unit(g, c, cg, v_evac)
                pend = (c, cg, a_kcs)
            pc, pcg, pa = pend
            kvmm_unit(g, pc, pcg, X, pa)
            if g == 3:
                gate_unit(g, pc, pcg)
        while side:
            side.pop(0)()

    def emit_kv_end():
        set_free(("xnfm", 0), "t")

    def emit_kvn_evac(blk, s):
        def vb(v):
            w(v, "t", C["t"])
            wait_free(v, ("kvnused", 0))
            v.tensor_copy(kvn[:, 0:455], pb[B_KV[0]][:, 0:455])
            v.tensor_copy(kvn[:, 455:910], pb[B_KV[1]][:, 0:455])
            v.tensor_copy(kvn[:, 910:1040], pb[B_KV[2]][:, 0:130])
            inc(LAST["v"], "v")
        blk.vector(_wrap("v", vb))
        v_kvn = C["v"]
        for bk in B_KV:
            kv_started[(s, bk)] = False
            set_free(("bank", bk), "v")

        def gb(g):
            w(g, "v", v_kvn)
            wait_free(g, ("kvnused", 0))
            inc(g.dma_start(out=kvn_sw[0:64, :], in_=kvn[64:128, :]), "skv", 16)
            inc(g.dma_start(out=kvn_sw[64:128, :], in_=kvn[0:64, :]), "skv", 16)
        blk.gpsimd(_wrap("g", gb))
        set_free(("kvnrdy", 0), "skv")

    B_Q = (6, 7)

    def emit_p2compute(blk, s, X, side=None, side_pre=None):
        for u in (side_pre or []):
            u()
        side = list(side) if side else []
        """Per window: a batch of expert slabs, then the attention window with
        Wq/Wout streamed through the same 3-slot slab ring (the PE is busy
        with q/Wout matmuls while those slots are occupied)."""
        ring = {"pos": 0}

        def next_slot():
            slotv = ring["pos"] % 3
            ring["pos"] += 1
            return slotv

        def exp_slab_units(j):
            e, fs = j // 2, j % 2
            slot = next_slot()
            wnm, wcnt = stream_pair(blk, slot, 8 + j)
            for c in range(NCH):
                cg = X * 8 + c
                pk = B_PRJ[c % 2]

                def tb(t, c=c, pk=pk, slot=slot, wnm=wnm, wcnt=wcnt):
                    w(t, wnm, wcnt)
                    wait_free(t, ("bank", pk))
                    wp = [whi[slot], wlo[slot]]
                    mm3(t, pb[pk][:],
                        lambda fc, wi, wp=wp: wp[wi][:, fc * 512:(fc + 1) * 512],
                        c * 128, True, True, X)
                    inc(LAST["t"], "t")
                blk.tensor(_wrap("t", tb))
                t_y = C["t"]
                if c >= 6:
                    set_free(("whi", slot), "t")
                    set_free(("wlo", slot), "t")

                def vb(v, e=e, fs=fs, c=c, cg=cg, pk=pk, t_y=t_y):
                    w(v, "t", t_y)
                    sl = acc[:, c * E + fs * 512:c * E + (fs + 1) * 512]
                    v.scalar_tensor_tensor(sl, pb[pk][:],
                                           masks[:, cg * 8 + e:cg * 8 + e + 1],
                                           sl, op0=ALU.mult, op1=ALU.add)
                    if c == NCH - 1:
                        v.drain()
                    inc(LAST["v"], "v")
                blk.vector(_wrap("v", vb))
                set_free(("bank", pk), "v")

        for wdx in range(4):
            gw = X * 4 + wdx
            nside = len(side) if wdx == 3 else (len(side) + 3 - wdx) // (4 - wdx)
            for _ in range(nside):
                side.pop(0)()
            for j in range(4 * wdx, 4 * wdx + 4):
                exp_slab_units(j)

            sq0, sq1 = next_slot(), next_slot()
            qn0, qv0 = stream_pair(blk, sq0, 4)
            qn1, qv1 = stream_pair(blk, sq1, 5)
            whiQ = [whi[sq0], whi[sq1]]
            wloQ = [wlo[sq0], wlo[sq1]]

            def strig(e, gw=gw):
                wait_free(e, ("trig", 0))
                off = gw * 256
                inc(e.dma_start(out=coswin[:],
                                in_=trig_h[0, off:off + 256].rearrange(
                                    "(a b) -> a b", a=1).partition_broadcast(128)),
                    "str", 16)
                inc(e.dma_start(out=sinwin[:],
                                in_=trig_h[1, off:off + 256].rearrange(
                                    "(a b) -> a b", a=1).partition_broadcast(128)),
                    "str", 16)
            blk.sync(strig)
            s_trig = C["str"]

            v_atp_d = {}

            def tail_ctx(qc, v_qcs):
                for hh in range(2):
                    h = 2 * qc + hh
                    cb = B_KV[hh]

                    def tc(t, qc=qc, hh=hh, h=h, cb=cb, v_qcs=v_qcs):
                        w(t, "v", v_qcs)
                        wait_free(t, ("bank", cb))
                        wait_free(t, ("kvnrdy", 0))
                        kc = kvn if hh == 0 else kvn_sw
                        ks = kvn_sw if hh == 0 else kvn
                        b0 = hh * 64
                        for ts in range(2):
                            t.matmul(pb[cb][0:128, ts * 65:ts * 65 + 65],
                                     qcos[qc % 2][b0:b0 + 64,
                                                  ts * 128:(ts + 1) * 128],
                                     kc[b0:b0 + 64, h * 65:h * 65 + 65],
                                     start=True, stop=False)
                            t.matmul(pb[cb][0:128, ts * 65:ts * 65 + 65],
                                     qsin[qc % 2][b0:b0 + 64,
                                                  ts * 128:(ts + 1) * 128],
                                     ks[b0:b0 + 64, h * 65:h * 65 + 65],
                                     start=False, stop=True)
                        inc(LAST["t"], "t")
                    blk.tensor(_wrap("t", tc))
                    t_ctx = C["t"]
                    if hh == 1:
                        set_free(("qcs", qc % 2), "t")

                    def vc(v, hh=hh, cb=cb, t_ctx=t_ctx):
                        w(v, "t", t_ctx)
                        pv = pb[cb][:, 0:130].rearrange("p (ts u) -> p ts u", u=65)[
                            :, :, 64:65]
                        nv = nrm4[:, hh * 4:hh * 4 + 4].rearrange(
                            "p (a u) -> p a u", u=2)[:, :, 0:1]
                        v.tensor_scalar_add(nv, pv, AT_EPS)
                        v.drain()
                        v.reciprocal(rcp4[:, hh * 4:hh * 4 + 4].rearrange(
                            "p (a u) -> p a u", u=2)[:, :, 0:1], nv)
                        v.drain()
                        inc(LAST["v"], "v")
                    blk.vector(_wrap("v", vc))

                    for ts in range(2):
                        def va(v, qc=qc, cb=cb, hh=hh, ts=ts):
                            if hh == 0:
                                wait_free(v, ("atile", qc % 2, ts))
                            v.tensor_scalar_mul(
                                atile[qc % 2][ts][:, hh * 64:hh * 64 + 64],
                                pb[cb][:, ts * 65:ts * 65 + 64],
                                rcp4[:, hh * 4 + ts * 2:hh * 4 + ts * 2 + 1])
                            inc(LAST["v"], "v")
                        blk.vector(_wrap("v", va))
                    free[("bank", cb)] = ("v", C["v"])
                v_atp_d[qc] = C["v"]

            def tail_atp(qc):
                v_atp = v_atp_d[qc]
                patp = B_TR[qc % 2]
                for ts in range(2):
                    def ta(t, qc=qc, ts=ts, patp=patp, v_atp=v_atp):
                        w(t, "v", v_atp)
                        if ts == 0:
                            wait_free(t, ("bank", patp))
                        t.transpose(pb[patp][0:128, ts * 128:(ts + 1) * 128],
                                    atile[qc % 2][ts][:], ident[:])
                        inc(LAST["t"], "t")
                    blk.tensor(_wrap("t", ta))
                    set_free(("atile", qc % 2, ts), "t")
                t_atp = C["t"]

                def aatp(a, qc=qc, patp=patp, t_atp=t_atp):
                    w(a, "t", t_atp)
                    wait_free(a, ("afm", qc))
                    a.activation(out=afm_hi[:, qc * 256:(qc + 1) * 256],
                                 in_=pb[patp][:, 0:256], func=AF.Copy)
                    inc(LAST["a"], "a")
                blk.scalar(_wrap("a", aatp))
                a_afm = C["a"]

                def vatp(v, qc=qc, patp=patp, t_atp=t_atp, a_afm=a_afm):
                    w(v, "t", t_atp)
                    w(v, "a", a_afm)
                    v.tensor_sub(afm_lo[:, qc * 256:(qc + 1) * 256],
                                 pb[patp][:, 0:256],
                                 afm_hi[:, qc * 256:(qc + 1) * 256])
                    inc(LAST["v"], "v")
                blk.vector(_wrap("v", vatp))
                set_free(("bank", patp), "v")
                free[("attnfm_rdy", qc)] = ("v", C["v"])

            pend_ctx = None
            pend_atp = None
            for qc in range(8):
                pq = B_Q[qc % 2]

                def tq(t, qc=qc, wdx=wdx, pq=pq):
                    w(t, qn0, qv0)
                    w(t, qn1, qv1)
                    wait_free(t, ("bank", pq))
                    n = 0
                    for (xi, wi) in [(0, 0), (0, 1), (1, 0)]:
                        for fc in range(8):
                            wp = [whiQ, wloQ][wi][fc // 4]
                            lhsT = wp[:, (fc % 4) * 1024 + qc * 128:
                                      (fc % 4) * 1024 + (qc + 1) * 128]
                            xs_ = [xnh[X], xnl[X]][xi]
                            rr = xs_[:, fc * HALF + wdx * 256:fc * HALF + (wdx + 1) * 256]
                            t.matmul(pb[pq][:, 0:256], lhsT, rr,
                                     start=(n == 0), stop=(n == 23))
                            n += 1
                    inc(LAST["t"], "t")
                blk.tensor(_wrap("t", tq))
                t_qp = C["t"]
                if qc == 7:
                    set_free(("whi", sq0), "t")
                    set_free(("wlo", sq0), "t")
                    set_free(("whi", sq1), "t")
                    set_free(("wlo", sq1), "t")

                def aq2(a, qc=qc, pq=pq, t_qp=t_qp):
                    w(a, "t", t_qp)
                    wait_free(a, ("qrelu", qc % 2))
                    a.activation(out=qrelu[qc % 2][:], in_=pb[pq][:, 0:256],
                                 func=AF.Relu, bias=bq_cols[:, qc:qc + 1])
                    inc(LAST["a"], "a")
                blk.scalar(_wrap("a", aq2))
                a_qr = C["a"]
                set_free(("bank", pq), "a")

                def vq2(v, qc=qc, a_qr=a_qr, s_trig=s_trig):
                    w(v, "a", a_qr)
                    w(v, "str", s_trig)
                    wait_free(v, ("qcs", qc % 2))
                    v.tensor_mul(qcos[qc % 2][:], qrelu[qc % 2][:], coswin[:])
                    v.tensor_mul(qsin[qc % 2][:], qrelu[qc % 2][:], sinwin[:])
                    inc(LAST["v"], "v")
                blk.vector(_wrap("v", vq2))
                v_qcs = C["v"]
                set_free(("qrelu", qc % 2), "v")
                if qc == 7:
                    set_free(("trig", 0), "v")

                if pend_ctx is not None:
                    tail_ctx(*pend_ctx)
                if pend_atp is not None:
                    tail_atp(pend_atp)
                pend_atp = pend_ctx[0] if pend_ctx is not None else None
                pend_ctx = (qc, v_qcs)
            tail_ctx(*pend_ctx)
            if pend_atp is not None:
                tail_atp(pend_atp)
            tail_atp(pend_ctx[0])

            so0_, so1_ = next_slot(), next_slot()
            on0, ov0 = stream_pair(blk, so0_, 6)
            on1, ov1 = stream_pair(blk, so1_, 7)
            whiO = [whi[so0_], whi[so1_]]
            wloO = [wlo[so0_], wlo[so1_]]
            for ts in range(2):
                for of in range(2):
                    pw = B_TR[(ts * 2 + of) % 2]

                    def tw(t, wdx=wdx, ts=ts, of=of, pw=pw):
                        w(t, on0, ov0)
                        w(t, on1, ov1)
                        for qc in range(8):
                            nm, val = free[("attnfm_rdy", qc)]
                            w(t, nm, val)
                        wait_free(t, ("bank", pw))
                        n = 0
                        for (xi, wi) in [(0, 0), (0, 1), (1, 0)]:
                            for ec in range(8):
                                af = [afm_hi, afm_lo][xi]
                                wp = [whiO, wloO][wi][ec // 4]
                                t.matmul(pb[pw][:],
                                         af[:, ec * 256 + ts * 128:ec * 256 + (ts + 1) * 128],
                                         wp[:, (ec % 4) * 1024 + of * 512:
                                            (ec % 4) * 1024 + (of + 1) * 512],
                                         start=(n == 0), stop=(n == 23))
                                n += 1
                        inc(LAST["t"], "t")
                    blk.tensor(_wrap("t", tw))
                    t_wo = C["t"]
                    if ts == 1 and of == 1:
                        set_free(("whi", so0_), "t")
                        set_free(("wlo", so0_), "t")
                        set_free(("whi", so1_), "t")
                        set_free(("wlo", so1_), "t")

                    def vw(v, wdx=wdx, ts=ts, of=of, pw=pw, t_wo=t_wo):
                        w(v, "t", t_wo)
                        sl = acc[:, (wdx * 2 + ts) * E + of * 512:
                                 (wdx * 2 + ts) * E + (of + 1) * 512]
                        v.tensor_add(sl, sl, pb[pw][:])
                        if ts == 1 and of == 1:
                            v.drain()
                        inc(LAST["v"], "v")
                    blk.vector(_wrap("v", vw))
                    set_free(("bank", pw), "v")
            for qc in range(8):
                free[("afm", qc)] = ("t", C["t"])
        set_free(("xnfm", X), "t")
        set_free(("kvnused", 0), "t")

    def out_units(blk, s, X):
        units = []

        def mk(c):
            def run():
                def vb(v, c=c):
                    wait_free(v, ("xbuf", c % 2))
                    if c == 0:
                        v.drain()
                    v.tensor_add(out_sb[c % 2][:], acc[:, c * E:(c + 1) * E],
                                 bout_bc[:])
                    inc(LAST["v"], "v")
                blk.vector(_wrap("v", vb))
                v_o = C["v"]

                def syb(e, c=c, v_o=v_o):
                    w(e, "v", v_o)
                    inc(e.dma_start(
                        out=out_h[s, X * HALF + c * 128:X * HALF + (c + 1) * 128, :],
                        in_=out_sb[c % 2][:]),
                        "so%d" % (c % 2), 16)
                blk.sync(syb)
                set_free(("xbuf", c % 2), "so%d" % (c % 2))
            return run

        for c in range(NCH):
            units.append(mk(c))
        return units

    def emit_out(blk, s, X):
        for u in out_units(blk, s, X):
            u()

    uA_next = None
    for s in range(2):
        uA = prep_units(blk, s, 0, hooks_in_b=False) if uA_next is None else uA_next
        if s == 0:
            for u in uA[:NCH]:
                u()
        uB = prep_units(blk, s, 1)
        emit_kv(blk, s, 0, side=(uA[NCH:] + uB) if s == 0 else uB)
        while presplit_pending:
            presplit_unit(presplit_pending.pop(0), 2)
        presplit_flush()
        if ("accrdy", 0) not in free:
            free[("accrdy", 0)] = ("sq0", C["sq0"])
            free[("accrdy", 1)] = ("sq1", C["sq1"])
        ai0 = accinit_units(blk, s, 0)
        emit_kv(blk, s, 1, side=ai0)
        emit_kvn_evac(blk, s)
        side1 = accinit_units(blk, s, 1)
        if s == 0:
            uA_next = prep_units(blk, s + 1, 0)
        emit_p2compute(blk, s, 0, side=(uA_next[:9] if s == 0 else None))
        emit_p2compute(blk, s, 1, side=(uA_next[9:] if s == 0 else None),
                       side_pre=out_units(blk, s, 0) + side1)
        if s == 0:
            uA_next = []
        emit_out(blk, s, 1)

    def syfin(e):
        w(e, "so0", C["so0"])
        w(e, "so1", C["so1"])
    blk.sync(syfin)

    blk_cm.__exit__(None, None, None)
    st.close()
    return nc


_NC_CACHE = None


def kernel(L0_attn_x, L0_attn_W_qkv, L0_attn_b_qkv, L0_attn_W_out, L0_attn_b_out,
           L0_ffn_gate_w, L0_ffn_expert_w):
    global _NC_CACHE
    x = np.ascontiguousarray(np.asarray(L0_attn_x, dtype=np.float32))
    wqkv = np.asarray(L0_attn_W_qkv, dtype=np.float32)
    gate = np.asarray(L0_ffn_gate_w, dtype=np.float32)
    wqkvg = np.ascontiguousarray(np.concatenate([wqkv, gate.T], axis=1))
    wout = np.ascontiguousarray(np.asarray(L0_attn_W_out, dtype=np.float32))
    bqkv = np.ascontiguousarray(np.asarray(L0_attn_b_qkv, dtype=np.float32))
    bout = np.ascontiguousarray(np.asarray(L0_attn_b_out, dtype=np.float32))
    wexp = np.ascontiguousarray(
        np.asarray(L0_ffn_expert_w, dtype=np.float32).transpose(0, 2, 1))
    with jax.default_device(jax.local_devices(backend="cpu")[0]):
        positions = jnp.arange(L, dtype=jnp.float32)
        trig = np.ascontiguousarray(np.stack([
            np.asarray(jnp.cos(jnp.pi / 2 * positions / L), dtype=np.float32),
            np.asarray(jnp.sin(jnp.pi / 2 * positions / L), dtype=np.float32)]))

    if _NC_CACHE is None:
        _NC_CACHE = build()
    nc = _NC_CACHE

    in_maps = []
    for i in range(8):
        in_maps.append({
            "x": x[2 * i:2 * i + 2],
            "wqkvg": wqkvg,
            "wout": wout,
            "bqkv": bqkv,
            "bout": bout,
            "wexp": wexp,
            "trig": trig,
        })
    res = run_bass_kernel_spmd(nc, in_maps, core_ids=list(range(8)))
    out = np.concatenate([r["out"] for r in res.results], axis=0)
    return out.reshape(16, 2048, 1024)
